# version 05 (final)
# baseline (speedup 1.0000x reference)
import contextlib
import numpy as np
import ml_dtypes
import concourse.bass as bass
import concourse.mybir as mybir
from concourse.bass_utils import run_bass_kernel_spmd

F32 = mybir.dt.float32
BF16 = mybir.dt.bfloat16
AF = mybir.ActivationFunctionType
ALU = mybir.AluOpType
AX = mybir.AxisListType

B, T, D = 2, 8192, 1024
TPC = 2048
NCORE = 8
BF = ml_dtypes.bfloat16


_ENGS = ("tensor", "vector", "scalar", "gpsimd", "sync")


def _ap_region(ap):
    t = ap.tensor
    shp = list(t.shape)
    if type(t).__name__ == "PSumTensorHandle":
        w = 1
        for s in shp[1:]:
            w *= s
        return (t.name, 0, shp[0], 0, max(w, 1))
    w = 1
    for s in shp[1:]:
        w *= s
    off = ap.offset
    dims = ap.ap
    if w > 0 and dims and dims[0][0] == w:
        p_lo = off // w
        f_lo = off % w
        p_hi = p_lo + dims[0][1]
        f_span = 1
        for st, ct in dims[1:]:
            f_span += abs(st) * (ct - 1)
        f_hi = f_lo + f_span
    else:
        p_lo, p_hi, f_lo, f_hi = 0, shp[0] if shp else 1, 0, max(w, 1)
    return (t.name, p_lo, p_hi, f_lo, f_hi)


def _overlap(a, b):
    return a[1] < b[2] and b[1] < a[2] and a[3] < b[4] and b[3] < a[4]


def _covers(a, b):
    return a[1] <= b[1] and a[2] >= b[2] and a[3] <= b[3] and a[4] >= b[4]


class Op:
    __slots__ = ("eng", "emit", "is_dma", "idx", "need_inc", "inc_no",
                 "waits", "tag", "is_out", "_epoch", "_wait_val")

    def __init__(self, eng, emit, is_dma, idx, tag):
        self.eng, self.emit, self.is_dma, self.idx = eng, emit, is_dma, idx
        self.need_inc = False
        self.inc_no = None
        self.waits = []
        self.tag = tag
        self.is_out = False


class Prog:
    def __init__(self, nc):
        self.nc = nc
        self.ops = {e: [] for e in _ENGS}
        self.writers = {}
        self.readers = {}
        self.n_edges = 0

    def op(self, eng, emit, reads=(), writes=(), is_dma=False, tag=""):
        o = Op(eng, emit, is_dma, len(self.ops[eng]), tag)
        rr = [_ap_region(a) for a in reads]
        ww = [_ap_region(a) for a in writes]
        deps = {}
        for r in rr:
            for reg, w in self.writers.get(r[0], ()):
                if _overlap(r, reg):
                    deps[id(w)] = w
        for wre in ww:
            for reg, w in self.writers.get(wre[0], ()):
                if _overlap(wre, reg):
                    deps[id(w)] = w
            for reg, rd in self.readers.get(wre[0], ()):
                if _overlap(wre, reg):
                    deps[id(rd)] = rd
        for d in deps.values():
            if d is o:
                continue
            if d.eng == "tensor" and o.eng == "tensor" and not d.is_dma and not o.is_dma:
                continue
            o.waits.append(d)
            d.need_inc = True
            self.n_edges += 1
        for wre in ww:
            lst = self.writers.setdefault(wre[0], [])
            lst[:] = [(reg, w) for reg, w in lst if not _covers(wre, reg)]
            lst.append((wre, o))
            lr = self.readers.get(wre[0])
            if lr:
                lr[:] = [(reg, rd) for reg, rd in lr if not _covers(wre, reg)]
        for r in rr:
            self.readers.setdefault(r[0], []).append((r, o))
        self.ops[eng].append(o)
        return o

    def dma(self, queue, out, in_, tag="", out_final=False, **kw):
        def emit(e, _o=out, _i=in_, _kw=kw):
            return e.dma_start(out=_o, in_=_i, **_kw)
        o = self.op(queue, emit, reads=[in_], writes=[out], is_dma=True, tag=tag)
        o.need_inc = True
        if out_final:
            o.is_out = True
        return o

    def finalize(self):
        nc = self.nc
        for e in _ENGS:
            nc_, nd_ = 0, 0
            for o in self.ops[e]:
                if o.need_inc:
                    if o.is_dma:
                        nd_ += 1
                        o.inc_no = nd_
                    else:
                        nc_ += 1
                        o.inc_no = nc_
        self.epoch_end = {}
        for e in _ENGS:
            epoch = 0
            total = 0
            end_of = {}
            dirty = False
            for o in self.ops[e]:
                if o.waits and dirty:
                    end_of[epoch] = total
                    epoch += 1
                    dirty = False
                if o.is_dma:
                    o._epoch = epoch
                    total += 1
                    dirty = True
            end_of[epoch] = total
            self.epoch_end[e] = end_of
            for o in self.ops[e]:
                if o.is_dma:
                    o._wait_val = 16 * end_of[o._epoch]
        stack = contextlib.ExitStack()
        sems = {}
        for e in _ENGS:
            sems[(e, "c")] = stack.enter_context(nc.semaphore(f"sc_{e}"))
            sems[(e, "d")] = stack.enter_context(nc.semaphore(f"sd_{e}"))
        blk = stack.enter_context(nc.Block())

        def run_stream(engobj, eng):
            known = {}
            last_out_dma = None
            cur_epoch = 0
            for o in self.ops[eng]:
                if o.is_dma and o._epoch > cur_epoch:
                    v = 16 * self.epoch_end[eng][o._epoch - 1]
                    if known.get((eng, "d"), 0) < v:
                        engobj.wait_ge(sems[(eng, "d")], v)
                        known[(eng, "d")] = v
                    cur_epoch = o._epoch
                for d in sorted(o.waits, key=lambda x: (x.eng, x.inc_no)):
                    kind = "d" if d.is_dma else "c"
                    val = d._wait_val if d.is_dma else d.inc_no
                    key = (d.eng, kind)
                    if known.get(key, 0) >= val:
                        continue
                    engobj.wait_ge(sems[key], val)
                    known[key] = val
                inst = o.emit(engobj)
                if o.need_inc:
                    kind = "d" if o.is_dma else "c"
                    inst.then_inc(sems[(o.eng, kind)], 16 if o.is_dma else 1)
                if o.is_out:
                    last_out_dma = o
            if last_out_dma is not None:
                tot = sum(1 for o in self.ops[eng] if o.is_dma and o.need_inc)
                engobj.wait_ge(sems[(eng, "d")], tot * 16)

        @blk.tensor
        def _(t):
            run_stream(t, "tensor")

        @blk.vector
        def _(v):
            run_stream(v, "vector")

        @blk.scalar
        def _(s):
            run_stream(s, "scalar")

        @blk.gpsimd
        def _(g):
            run_stream(g, "gpsimd")

        @blk.sync
        def _(sy):
            run_stream(sy, "sync")

        stack.close()
        return nc


class ConstPool:

    def __init__(self, nc, sb):
        self.nc = nc
        self.sb = sb
        self.vals = []

    def col(self, v, parts=128):
        v = float(v)
        if v in self.vals:
            i = self.vals.index(v)
        else:
            i = len(self.vals)
            self.vals.append(v)
            self.nc.const_aps.register(F32, v, self.sb[:, i:i + 1])
        return self.sb[0:parts, i:i + 1]


def p_mm(P, out, lhsT, rhs, start=True, stop=True, tag="mm"):
    def emit(e, o=out, l=lhsT, r=rhs, st=start, sp=stop):
        return e.matmul(o, l, r, start=st, stop=sp)
    return P.op("tensor", emit, reads=[lhsT, rhs], writes=[out], tag=tag)


def p_trans(P, out, in_, ident, tag="tr"):
    def emit(e, o=out, i=in_, d=ident):
        return e.transpose(o, i, d)
    return P.op("tensor", emit, reads=[in_, ident], writes=[out], tag=tag)


def p_act(P, out, in_, func, bias=0.0, scale=1.0, accum_out=None, tag="act"):
    reads = [in_]
    writes = [out]
    if hasattr(bias, "tensor"):
        reads.append(bias)
    if hasattr(scale, "tensor"):
        reads.append(scale)
    if accum_out is not None:
        writes.append(accum_out)

    def emit(e, o=out, i=in_, f=func, b=bias, s=scale, a=accum_out):
        if a is None:
            return e.activation(o, i, f, bias=b, scale=s)
        return e.activation(o, i, f, bias=b, scale=s, accum_out=a)
    return P.op("scalar", emit, reads=reads, writes=writes, tag=tag)


def p_sadd(P, out, in_, addv, tag="sadd"):
    reads = [in_] + ([addv] if hasattr(addv, "tensor") else [])

    def emit(e, o=out, i=in_, a=addv):
        return e.add(o, i, a)
    return P.op("scalar", emit, reads=reads, writes=[out], tag=tag)


def p_smul(P, out, in_, mulv, tag="smul"):
    reads = [in_] + ([mulv] if hasattr(mulv, "tensor") else [])

    def emit(e, o=out, i=in_, m=mulv):
        return e.mul(o, i, m)
    return P.op("scalar", emit, reads=reads, writes=[out], tag=tag)


def p_copy(P, eng, out, in_, tag="cp"):
    if eng == "scalar":
        def emit(e, o=out, i=in_):
            return e.copy(o, i)
    else:
        def emit(e, o=out, i=in_):
            return e.tensor_copy(o, i)
    return P.op(eng, emit, reads=[in_], writes=[out], tag=tag)


def p_tt(P, out, a, b, op, eng="vector", tag="tt"):
    def emit(e, o=out, x=a, y=b, p=op):
        return e.tensor_tensor(o, x, y, p)
    return P.op(eng, emit, reads=[a, b], writes=[out], tag=tag)


def p_ts(P, out, in0, s1, s2, op0, op1=None, eng="vector", accum_out=None, tag="ts"):
    reads = [in0]
    writes = [out]
    for s in (s1, s2):
        if hasattr(s, "tensor"):
            reads.append(s)
    if accum_out is not None:
        writes.append(accum_out)

    def emit(e, o=out, i=in0, a=s1, b=s2, p0=op0, p1=op1, ac=accum_out):
        kw = {}
        if p1 is not None:
            kw["op1"] = p1
        if ac is not None:
            kw["accum_out"] = ac
        return e.tensor_scalar(o, i, a, b, p0, **kw)
    return P.op(eng, emit, reads=reads, writes=writes, tag=tag)


def p_stt(P, out, in0, scalar, in1, op0, op1, eng="vector", tag="stt"):
    reads = [in0, in1]
    if hasattr(scalar, "tensor"):
        reads.append(scalar)

    def emit(e, o=out, i=in0, s=scalar, j=in1, p0=op0, p1=op1):
        return e.scalar_tensor_tensor(o, i, s, j, p0, p1)
    return P.op(eng, emit, reads=reads, writes=[out], tag=tag)


def p_red(P, out, in_, op, negate=False, absv=False, tag="red"):
    def emit(e, o=out, i=in_, p=op, n=negate, a=absv):
        return e.tensor_reduce(o, i, AX.X, p, negate=n, apply_absolute_value=a)
    return P.op("vector", emit, reads=[in_], writes=[out], tag=tag)


def p_memset(P, eng, ap, v, tag="ms"):
    def emit(e, a=ap, x=v):
        return e.memset(a, x)
    return P.op(eng, emit, reads=[], writes=[ap], tag=tag)


def p_recip(P, out, in_, tag="recip"):
    def emit(e, o=out, i=in_):
        return e.reciprocal(o, i)
    return P.op("vector", emit, reads=[in_], writes=[out], tag=tag)


def feat_rows(nc, P, name, dram_ap, n, stack, dt=F32):
    k = n // 128
    sb = stack.enter_context(nc.sbuf_tensor(name, [128, k], dt))
    src = dram_ap.rearrange("(c p) -> p c", p=128)
    P.dma("scalar", sb[:], src, tag=f"ld_{name}", allow_slow_non_contiguous=True)
    return sb


def build_launch_A():
    nc = bass.Bass("TRN2", target_bir_lowering=False)
    P = Prog(nc)
    stack = contextlib.ExitStack()
    sb = lambda n, s, d=F32: stack.enter_context(nc.sbuf_tensor(n, s, d))
    ps = lambda n, s, d=F32: stack.enter_context(nc.psum_tensor(n, s, d))

    xT = nc.dram_tensor("xT", [D, TPC], F32, kind="ExternalInput").ap()
    Wqkv = nc.dram_tensor("Wqkv", [D, 3 * D], F32, kind="ExternalInput").ap()
    bqkv_d = nc.dram_tensor("bqkv", [3 * D], F32, kind="ExternalInput").ap()
    gq_d = nc.dram_tensor("gq", [D], F32, kind="ExternalInput").ap()
    bq_d = nc.dram_tensor("bq", [D], F32, kind="ExternalInput").ap()
    gk_d = nc.dram_tensor("gk", [D], F32, kind="ExternalInput").ap()
    bk_d = nc.dram_tensor("bk", [D], F32, kind="ExternalInput").ap()
    consts_d = nc.dram_tensor("consts", [128, 8], F32, kind="ExternalInput").ap()
    crow_d = nc.dram_tensor("crow", [1, 128], F32, kind="ExternalInput").ap()
    kerq_o = nc.dram_tensor("kerq", [D, TPC], BF16, kind="ExternalOutput").ap()
    kerk_o = nc.dram_tensor("kerk", [D, TPC], BF16, kind="ExternalOutput").ap()
    v_o = nc.dram_tensor("v", [D, TPC], BF16, kind="ExternalOutput").ap()

    x_sb = sb("x_sb", [128, 8 * TPC])
    wq_sb = sb("wq_sb", [128, 8 * 3 * D], BF16)
    cst = sb("cst", [128, 8])
    crow = sb("crow_sb", [1, 128])
    cp = ConstPool(nc, cst)
    bqkv = feat_rows(nc, P, "bqkv_sb", bqkv_d, 3 * D, stack)
    gq = feat_rows(nc, P, "gq_sb", gq_d, D, stack)
    bq = feat_rows(nc, P, "bq_sb", bq_d, D, stack)
    gk = feat_rows(nc, P, "gk_sb", gk_d, D, stack)
    bk = feat_rows(nc, P, "bk_sb", bk_d, D, stack)
    P.dma("scalar", cst[:], consts_d, tag="ld_cst")
    P.dma("scalar", crow[:], crow_d, tag="ld_crow")
    for c in range(8):
        P.dma("sync", x_sb[:, c * TPC:(c + 1) * TPC], xT[c * 128:(c + 1) * 128, :],
              tag=f"ld_x{c}")
    for kc in range(8):
        P.dma("gpsimd", wq_sb[:, kc * 3072:(kc + 1) * 3072],
              Wqkv[kc * 128:(kc + 1) * 128, :], tag=f"ld_w{kc}")

    W = 512
    sq = [sb(f"sq{i}", [128, W]) for i in range(2)]
    rows = sb("rows", [1, 16 * 8])
    r_m = sb("r_m", [1, W]); r_ss = sb("r_ss", [1, W]); r_a = sb("r_a", [1, W])
    r_b = sb("r_b", [1, W]); r_t = sb("r_t", [1, W])
    A2 = sb("A2", [128, W]); B2 = sb("B2", [128, W])
    xn_bf = sb("xn_bf", [128, 8 * W], BF16)
    qkv_sb = sb("qkv_sb", [128, 24 * W])
    t1 = sb("t1", [128, W]); t2 = sb("t2", [128, W])
    ker_bf = sb("ker_bf", [128, W], BF16)
    p_sum = ps("p_sum", [1, W]); p_sq = ps("p_sq", [1, W])
    pA = ps("pA", [128, W]); pB = ps("pB", [128, W])
    pq = [ps(f"pq{i}", [128, W]) for i in range(2)]
    ones_col = cp.col(1.0)

    def stats(chunks_ap, n, ddof):
        for i, c_ap in enumerate(chunks_ap):
            p_mm(P, p_sum[:], ones_col, c_ap, start=(i == 0), stop=(i == len(chunks_ap) - 1))
        for i, c_ap in enumerate(chunks_ap):
            s = sq[i % 2]
            p_act(P, s[:], c_ap, AF.Square)
            p_mm(P, p_sq[:], ones_col, s[:], start=(i == 0), stop=(i == len(chunks_ap) - 1))
        p_act(P, r_m[:], p_sum[0:1, :], AF.Copy, scale=1.0 / n)
        p_act(P, r_t[:], r_m[:], AF.Square)
        p_stt(P, r_ss[:], r_t[:], cp.col(-float(n), 1), p_sq[0:1, :], ALU.mult, ALU.add)
        if ddof == 1:
            p_act(P, r_t[:], r_ss[:], AF.Sqrt, scale=1.0 / (n - 1))
            p_ts(P, r_t[:], r_t[:], cp.col(1e-6, 1), None, ALU.add)
            p_recip(P, r_a[:], r_t[:])
        else:
            p_act(P, r_t[:], r_ss[:], AF.Sqrt, scale=1.0 / n, bias=cp.col(1e-5, 1))
            p_recip(P, r_a[:], r_t[:])
        p_stt(P, r_b[:], r_m[:], cp.col(-1.0, 1), r_a[:], ALU.mult, ALU.mult)
        p_mm(P, pA[:], crow[:], r_a[:])
        p_mm(P, pB[:], crow[:], r_b[:])
        p_copy(P, "vector", A2[:], pA[:])
        p_copy(P, "vector", B2[:], pB[:])

    NT = TPC // W
    for t in range(NT):
        tok = slice(t * W, (t + 1) * W)
        xc = [x_sb[:, c * TPC + t * W: c * TPC + t * W + W] for c in range(8)]
        stats(xc, D, ddof=0)
        for c in range(8):
            p_tt(P, t1[:], xc[c], A2[:], ALU.mult)
            p_tt(P, t2[:], t1[:], B2[:], ALU.add)
            p_copy(P, "scalar", xn_bf[:, c * W:(c + 1) * W], t2[:])
        for mc in range(24):
            bank = pq[mc % 2]
            for kc in range(8):
                p_mm(P, bank[:], wq_sb[:, kc * 3072 + mc * 128: kc * 3072 + mc * 128 + 128],
                     xn_bf[:, kc * W:(kc + 1) * W],
                     start=(kc == 0), stop=(kc == 7))
            p_sadd(P, qkv_sb[:, mc * W:(mc + 1) * W], bank[:], bqkv[:, mc:mc + 1])
        for which, g_t, b_t, out_d in (("q", gq, bq, kerq_o), ("k", gk, bk, kerk_o)):
            off = 0 if which == "q" else 8
            qc = [qkv_sb[:, (off + c) * W:(off + c + 1) * W] for c in range(8)]
            stats(qc, D, ddof=1)
            for c in range(8):
                p_tt(P, t1[:], qc[c], A2[:], ALU.mult)
                p_ts(P, t2[:], t1[:], g_t[:, c:c + 1], b_t[:, c:c + 1], ALU.mult, ALU.add)
                p_stt(P, t1[:], B2[:], g_t[:, c:c + 1], t2[:], ALU.mult, ALU.add)
                p_act(P, ker_bf[:], t1[:], AF.Square)
                P.dma("sync", out_d[c * 128:(c + 1) * 128, tok], ker_bf[:],
                      tag=f"st_{which}{c}", out_final=True)
        for c in range(8):
            P.dma("gpsimd", v_o[c * 128:(c + 1) * 128, tok],
                  qkv_sb[:, (16 + c) * W:(17 + c) * W], tag=f"st_v{c}", out_final=True)

    nc2 = P.finalize()
    stack.close()
    return nc2, cp.vals


def p_mmtp(P, out, lhsT, rhs, start, stop, tp, tag="mmtp"):
    def emit(e, o=out, l=lhsT, r=rhs, st=start, sp=stop, t=tp):
        return e.matmul(o, l, r, start=st, stop=sp, tile_position=t)
    return P.op("tensor", emit, reads=[lhsT, rhs], writes=[out], tag=tag)


def build_launch_B():
    nc = bass.Bass("TRN2", target_bir_lowering=False)
    P = Prog(nc)
    stack = contextlib.ExitStack()
    sb = lambda n, s, d=F32: stack.enter_context(nc.sbuf_tensor(n, s, d))
    ps = lambda n: stack.enter_context(nc.psum_tensor(n, [128, 512], F32))

    kq_d = nc.dram_tensor("kq", [2, 128, T], BF16, kind="ExternalInput").ap()
    kk_d = nc.dram_tensor("kk", [2, 128, T], BF16, kind="ExternalInput").ap()
    kkT_d = nc.dram_tensor("kkT", [2, T, 128], BF16, kind="ExternalInput").ap()
    vT_d = nc.dram_tensor("vT", [2, T, 128], BF16, kind="ExternalInput").ap()
    cmask_d = nc.dram_tensor("cmask", [128, 128], F32, kind="ExternalInput").ap()
    consts_d = nc.dram_tensor("consts", [128, 8], F32, kind="ExternalInput").ap()
    crow_d = nc.dram_tensor("crow", [1, 128], F32, kind="ExternalInput").ap()
    onesb_d = nc.dram_tensor("onesb", [128, 1], BF16, kind="ExternalInput").ap()
    o_out = nc.dram_tensor("o", [2, 128, T], BF16, kind="ExternalOutput").ap()

    kq = sb("kq_sb", [128, 2 * T], BF16)
    kk = sb("kk_sb", [128, 2 * T], BF16)
    kkT = sb("kkT_sb", [128, 2 * T], BF16)
    vT = sb("vT_sb", [128, 2 * T], BF16)
    cmask = sb("cmask_sb", [128, 128])
    cst = sb("cst", [128, 8])
    crow = sb("crow_sb", [1, 128])
    onesb = sb("onesb_sb", [128, 1], BF16)
    cp = ConstPool(nc, cst)
    S_st = sb("S_st", [128, 2 * 64])
    S_bf = sb("S_bf", [128, 2 * 64], BF16)
    s_st = sb("s_st", [128, 2])
    s_bf = sb("s_bf", [128, 2], BF16)
    zr = sb("zr", [1, 128])
    zb_sb = sb("zb_sb", [128, 128])
    atm = [sb(f"atm{i}", [128, 128], BF16) for i in range(2)]
    o_acc = sb("o_acc", [128, T], BF16)

    P.dma("scalar", cst[:], consts_d)
    P.dma("scalar", crow[:], crow_d)
    P.dma("scalar", cmask[:], cmask_d)
    P.dma("scalar", onesb[:], onesb_d)
    for p in range(2):
        P.dma("sync", kq[:, p * T:(p + 1) * T], kq_d[p], tag=f"ld_kq{p}")
        P.dma("sync", kk[:, p * T:(p + 1) * T], kk_d[p], tag=f"ld_kk{p}")
        src_t = kkT_d[p].rearrange("(n p2) c -> p2 n c", p2=128)
        dst_t = kkT[:, p * T:(p + 1) * T].rearrange("p2 (n c) -> p2 n c", c=128)
        P.dma("sync", dst_t, src_t, tag=f"ld_kkT{p}")
        src_v = vT_d[p].rearrange("(n p2) c -> p2 n c", p2=128)
        dst_v = vT[:, p * T:(p + 1) * T].rearrange("p2 (n c) -> p2 n c", c=128)
        P.dma("sync", dst_v, src_v, tag=f"ld_vT{p}")
    p_memset(P, "vector", S_st[:], 0.0)
    p_memset(P, "vector", s_st[:], 0.0)
    p_memset(P, "vector", S_bf[:], 0.0)
    p_memset(P, "vector", s_bf[:], 0.0)

    PS_at = [ps("PS_at0"), ps("PS_at1")]
    PS_o = [ps("PS_o0"), ps("PS_o1")]
    PS_z = ps("PS_z")
    PS_zb = ps("PS_zb")
    PS_s = ps("PS_s")
    NCH = T // 128

    for p in range(2):
        for c in range(NCH):
            ch = slice(p * T + c * 128, p * T + (c + 1) * 128)
            for h in range(2):
                hp = slice(64 * h, 64 * h + 64)
                hc = slice(64 * h, 64 * h + 64)
                pa = PS_at[(c * 2 + h) % 2]
                po = PS_o[(c * 2 + h) % 2]
                Scol = slice(64 * p, 64 * p + 64)
                p_mm(P, pa[:, 0:128], kk[hp, ch], kq[hp, ch], tag="at")
                p_tt(P, atm[h][:], pa[:, 0:128], cmask[:], ALU.mult, tag="atm")
                p_mmtp(P, po[hp, 0:128], vT[:, ch][:, hc], atm[h][:], True, False,
                       (0, 64 * h))
                p_mmtp(P, po[hp, 0:128], S_bf[hp, Scol], kq[hp, ch], False, True,
                       (64 * h, 64 * h))
                p_mm(P, PS_z[0:1, 0:128], onesb[:], atm[h][:], start=True, stop=False,
                     tag="z1")
                p_mm(P, PS_z[0:1, 0:128], s_bf[hp, p:p + 1], kq[hp, ch],
                     start=False, stop=True, tag="z2")
                p_ts(P, zr[:], PS_z[0:1, 0:128], cp.col(1e-6, 1), None, ALU.add, tag="zeps")
                p_recip(P, zr[:], zr[:])
                p_mmtp(P, PS_zb[hp, 0:128], crow[0:1, 0:64], zr[:], True, True,
                       (0, 64 * h))
                p_copy(P, "scalar", zb_sb[hp, :], PS_zb[hp, 0:128])
                p_tt(P, o_acc[hp, c * 128:(c + 1) * 128], po[hp, 0:128], zb_sb[hp, :],
                     ALU.mult, tag="ozb")
                p_mmtp(P, PS_s[hp, 0:64], kkT[:, ch][:, hc], vT[:, ch][:, hc],
                       True, True, (0, 64 * h))
                p_mmtp(P, PS_s[hp, 64:65], kkT[:, ch][:, hc], onesb[:], True, True,
                       (0, 64 * h))
                p_tt(P, S_st[hp, Scol], S_st[hp, Scol], PS_s[hp, 0:64], ALU.add, tag="Supd")
                p_tt(P, s_st[hp, p:p + 1], s_st[hp, p:p + 1], PS_s[hp, 64:65],
                     ALU.add, tag="supd")
                p_copy(P, "scalar", S_bf[hp, Scol], S_st[hp, Scol], tag="Sbf")
                p_copy(P, "scalar", s_bf[hp, p:p + 1], s_st[hp, p:p + 1], tag="sbf")
        P.dma("sync", o_out[p], o_acc[:], tag=f"st_o{p}", out_final=True)
    nc2 = P.finalize()
    stack.close()
    return nc2, cp.vals


def emit_stats(P, cp, crow, chunks, n, ddof, scr, eps):
    nc_ = len(chunks)
    for i, c_ap in enumerate(chunks):
        p_mm(P, scr["PS_sum"], cp.col(1.0), c_ap, start=(i == 0), stop=(i == nc_ - 1))
    for i, c_ap in enumerate(chunks):
        s = scr["sq0"] if i % 2 == 0 else scr["sq1"]
        p_act(P, s[:], c_ap, AF.Square)
        p_mm(P, scr["PS_sq"], cp.col(1.0), s[:], start=(i == 0), stop=(i == nc_ - 1))
    p_act(P, scr["r_m"][:], scr["PS_sum"][0:1, :], AF.Copy, scale=1.0 / n)
    p_act(P, scr["r_t"][:], scr["r_m"][:], AF.Square)
    p_stt(P, scr["r_ss"][:], scr["r_t"][:], cp.col(-float(n), 1), scr["PS_sq"][0:1, :],
          ALU.mult, ALU.add)
    if ddof == 1:
        p_act(P, scr["r_t"][:], scr["r_ss"][:], AF.Sqrt, scale=1.0 / (n - 1))
        p_ts(P, scr["r_t"][:], scr["r_t"][:], cp.col(eps, 1), None, ALU.add)
        p_recip(P, scr["r_a"][:], scr["r_t"][:])
    else:
        p_act(P, scr["r_t"][:], scr["r_ss"][:], AF.Sqrt, scale=1.0 / n,
              bias=cp.col(eps, 1))
        p_recip(P, scr["r_a"][:], scr["r_t"][:])
    p_stt(P, scr["r_b"][:], scr["r_m"][:], cp.col(-1.0, 1), scr["r_a"][:],
          ALU.mult, ALU.mult)
    p_mm(P, scr["PS_A"], crow[:], scr["r_a"][:])
    p_mm(P, scr["PS_B"], crow[:], scr["r_b"][:])
    p_copy(P, "vector", scr["A2"][:], scr["PS_A"])
    p_copy(P, "vector", scr["B2"][:], scr["PS_B"])


def build_launch_C(debug=False, phases=5):
    nc = bass.Bass("TRN2", target_bir_lowering=False)
    P = Prog(nc)
    stack = contextlib.ExitStack()
    sb = lambda n, s, d=F32: stack.enter_context(nc.sbuf_tensor(n, s, d))
    psf = lambda n: stack.enter_context(nc.psum_tensor(n, [128, 512], F32))

    xT_d = nc.dram_tensor("xT", [D, TPC], F32, kind="ExternalInput").ap()
    oT_d = nc.dram_tensor("oT", [D, TPC], BF16, kind="ExternalInput").ap()
    Wo_d = nc.dram_tensor("Wo", [D, D], F32, kind="ExternalInput").ap()
    bo_d = nc.dram_tensor("bo", [D], F32, kind="ExternalInput").ap()
    gwT_d = nc.dram_tensor("gwT", [D, 8], F32, kind="ExternalInput").ap()
    gb_d = nc.dram_tensor("gb", [1, 8], F32, kind="ExternalInput").ap()
    eb_d = nc.dram_tensor("eb", [1, 8], F32, kind="ExternalInput").ap()
    w1_d = nc.dram_tensor("w1", [8, D, 4096], F32, kind="ExternalInput").ap()
    b1_d = nc.dram_tensor("b1", [8, 4096], F32, kind="ExternalInput").ap()
    w2_d = nc.dram_tensor("w2", [8, 4096, D], F32, kind="ExternalInput").ap()
    b2_d = nc.dram_tensor("b2", [8, D], F32, kind="ExternalInput").ap()
    Wq_d = nc.dram_tensor("Wq", [D, D], F32, kind="ExternalInput").ap()
    Wk_d = nc.dram_tensor("Wk", [D, 3 * D], F32, kind="ExternalInput").ap()
    Wv_d = nc.dram_tensor("Wv", [D, 3 * D], F32, kind="ExternalInput").ap()
    Wb_d = nc.dram_tensor("Wb", [D, 24], F32, kind="ExternalInput").ap()
    Wa_d = nc.dram_tensor("Wa", [D, 8], F32, kind="ExternalInput").ap()
    Wg_d = nc.dram_tensor("Wg", [D, D], F32, kind="ExternalInput").ap()
    ident_d = nc.dram_tensor("ident", [128, 128], F32, kind="ExternalInput").ap()
    consts_d = nc.dram_tensor("consts", [128, 8], F32, kind="ExternalInput").ap()
    crow_d = nc.dram_tensor("crow", [1, 128], F32, kind="ExternalInput").ap()
    xo_d = nc.dram_tensor("x_out", [D, TPC], F32, kind="ExternalOutput").ap()
    uq_d = nc.dram_tensor("uq", [D, TPC], BF16, kind="ExternalOutput").ap()
    uk_d = nc.dram_tensor("uk", [3 * D, TPC], BF16, kind="ExternalOutput").ap()
    uv_d = nc.dram_tensor("uv", [3 * D, TPC], BF16, kind="ExternalOutput").ap()
    ug_d = nc.dram_tensor("ug", [D, TPC], BF16, kind="ExternalOutput").ap()
    ub_d = nc.dram_tensor("ub", [TPC, 24], F32, kind="ExternalOutput").ap()
    ua_d = nc.dram_tensor("ua", [TPC, 8], F32, kind="ExternalOutput").ap()

    x_sb = sb("x_sb", [128, 8 * TPC])
    xn_bf = sb("xn_bf", [128, 8 * TPC], BF16)
    xn32 = sb("xn32", [128, 8 * 128])
    wslot = [sb(f"wslot{i}", [128, 16384], BF16) for i in range(2)]
    h_bf = sb("h_bf", [128, 8 * 256], BF16)
    y_acc = sb("y_acc", [128, 8 * 1024], BF16)
    b2bc = sb("b2bc", [128, 1024])
    b2row = sb("b2row", [1, 1024])
    b1e = sb("b1e", [128, 32])
    mask_all = sb("mask_all", [128, 16 * 8])
    cst = sb("cst", [128, 8])
    crow = sb("crow_sb", [1, 128])
    identb = sb("identb", [128, 128], BF16)
    cp = ConstPool(nc, cst)
    bo = feat_rows(nc, P, "bo_sb", bo_d, D, stack)
    gwT = sb("gwT_sb", [128, 64])
    gbias = sb("gbias", [128, 8])
    logits = sb("logits", [128, 8])
    m8 = sb("m8", [128, 1])
    ub_sb = sb("ub_sb", [128, 16 * 24])
    ua_sb = sb("ua_sb", [128, 16 * 8])
    obuf = [sb(f"obuf{i}", [128, 512], BF16) for i in range(2)]
    W = 256
    scr = dict(sq0=sb("sq0", [128, W]), sq1=sb("sq1", [128, W]),
               r_m=sb("r_m", [1, W]), r_ss=sb("r_ss", [1, W]),
               r_a=sb("r_a", [1, W]), r_b=sb("r_b", [1, W]),
               r_t=sb("r_t", [1, W]),
               A2=sb("A2", [128, W]), B2=sb("B2", [128, W]))
    wb_sb = sb("wb_sb", [128, 8 * 24])
    wa_sb2 = sb("wa_sb2", [128, 8 * 8])

    PS_y = [psf(f"PS_y{i}") for i in range(4)]
    PS_h = [psf(f"PS_h{i}") for i in range(2)]
    PS_m = psf("PS_m")
    PS_tb = stack.enter_context(nc.psum_tensor("PS_tb", [128, 512], BF16))

    P.dma("scalar", cst[:], consts_d)
    P.dma("scalar", crow[:], crow_d)
    P.dma("gpsimd", identb[:], ident_d)
    P.dma("scalar", gwT[:].rearrange("p (kc e) -> p kc e", e=8),
          gwT_d.rearrange("(kc p) e -> p kc e", p=128),
          allow_slow_non_contiguous=True)
    for c in range(8):
        P.dma("sync", x_sb[:, c * TPC:(c + 1) * TPC], xT_d[c * 128:(c + 1) * 128, :])
    gbrow = sb("gbrow", [1, 8])
    ebrow = sb("ebrow", [1, 8])
    P.dma("scalar", gbrow[:], gb_d)
    P.dma("scalar", ebrow[:], eb_d)
    p_mm(P, PS_m[:, 0:8], crow[:], gbrow[:], start=True, stop=False)
    p_mm(P, PS_m[:, 0:8], crow[:], ebrow[:], start=False, stop=True)
    p_copy(P, "vector", gbias[:], PS_m[:, 0:8])

    for kc in range(8):
        P.dma("gpsimd", wslot[0][:, kc * 1024:(kc + 1) * 1024],
              Wo_d[kc * 128:(kc + 1) * 128, :], tag=f"ld_wo{kc}")
    o_sb = xn_bf
    for c in range(8):
        P.dma("sync", o_sb[:, c * TPC:(c + 1) * TPC], oT_d[c * 128:(c + 1) * 128, :])
    for t in range(4):
        for mc in range(8):
            bank = PS_y[mc % 4][:, 0:512]
            for kc in range(8):
                p_mm(P, bank, wslot[0][:, kc * 1024 + mc * 128: kc * 1024 + mc * 128 + 128],
                     o_sb[:, kc * TPC + t * 512: kc * TPC + t * 512 + 512],
                     start=(kc == 0), stop=(kc == 7))
            xs = x_sb[:, mc * TPC + t * 512: mc * TPC + t * 512 + 512]
            p_stt(P, xs, bank, bo[:, mc:mc + 1], xs, ALU.add, ALU.add, tag="resid0")

    def ln_phase(gate):
        for t8 in range(8):
            xc = [x_sb[:, c * TPC + t8 * 256: c * TPC + t8 * 256 + 256] for c in range(8)]
            scr["PS_sum"] = PS_h[0][0:1, 0:256]
            scr["PS_sq"] = PS_h[1][0:1, 0:256]
            scr["PS_A"] = PS_y[0][:, 0:256]
            scr["PS_B"] = PS_y[1][:, 0:256]
            emit_stats(P, cp, crow, xc, D, 0, scr, 1e-5)
            for s in range(2):
                st8 = t8 * 2 + s
                ssl = slice(s * 128, s * 128 + 128)
                for c in range(8):
                    x32 = xn32[:, c * 128:(c + 1) * 128]
                    p_tt(P, scr["sq0"][:, 0:128], xc[c][:, ssl], scr["A2"][:, ssl],
                         ALU.mult)
                    p_tt(P, x32, scr["sq0"][:, 0:128], scr["B2"][:, ssl], ALU.add)
                    p_copy(P, "scalar",
                           xn_bf[:, c * TPC + st8 * 128: c * TPC + st8 * 128 + 128], x32)
                if gate == "gate":
                    for kc in range(8):
                        p_mm(P, PS_m[:, 16:24], xn32[:, kc * 128:(kc + 1) * 128],
                             gwT[:, kc * 8:(kc + 1) * 8], start=(kc == 0),
                             stop=(kc == 7), tag="gate")
                    p_tt(P, logits[:], PS_m[:, 16:24], gbias[:], ALU.add)
                    p_red(P, m8[:], logits[:], ALU.max)
                    p_ts(P, mask_all[:, st8 * 8:(st8 + 1) * 8], logits[:],
                         m8[:, 0:1], None, ALU.is_ge)
                elif gate == "ba":
                    for kc in range(8):
                        p_mm(P, PS_m[:, 0:24], xn32[:, kc * 128:(kc + 1) * 128],
                             wb_sb[:, kc * 24:(kc + 1) * 24], start=(kc == 0),
                             stop=(kc == 7), tag="ubm")
                        p_mm(P, PS_y[2][:, 0:8], xn32[:, kc * 128:(kc + 1) * 128],
                             wa_sb2[:, kc * 8:(kc + 1) * 8], start=(kc == 0),
                             stop=(kc == 7), tag="uam")
                    p_copy(P, "vector", ub_sb[:, st8 * 24:(st8 + 1) * 24], PS_m[:, 0:24])
                    p_copy(P, "vector", ua_sb[:, st8 * 8:(st8 + 1) * 8], PS_y[2][:, 0:8])

    ln_phase("gate")
    if debug:
        dbg_x1 = nc.dram_tensor("dbg_x1", [D, TPC], F32, kind="ExternalOutput").ap()
        for c in range(8):
            P.dma("sync", dbg_x1[c * 128:(c + 1) * 128, :],
                  x_sb[:, c * TPC:(c + 1) * TPC], tag="dbgx1", out_final=True)
        dbg_mask = nc.dram_tensor("dbg_mask", [128, 16 * 8], F32,
                                  kind="ExternalOutput").ap()
        P.dma("sync", dbg_mask, mask_all[:], tag="dbgmask", out_final=True)
    if phases < 3:
        nc2 = P.finalize()
        stack.close()
        return nc2, cp.vals

    def load_quarter(e, q, slot):
        for kc in range(8):
            P.dma("gpsimd", wslot[slot][:, kc * 1024:(kc + 1) * 1024],
                  w1_d[e, kc * 128:(kc + 1) * 128, q * 1024:(q + 1) * 1024],
                  tag=f"w1 e{e}q{q}")
        for kc in range(8):
            P.dma("gpsimd", wslot[slot][:, 8192 + kc * 1024: 8192 + (kc + 1) * 1024],
                  w2_d[e, q * 1024 + kc * 128: q * 1024 + (kc + 1) * 128, :],
                  tag=f"w2 e{e}q{q}")

    load_quarter(0, 0, 1)
    seq = 0
    for g in range(2):
        p_memset(P, "vector", y_acc[:], 0.0)
        for e in range(8):
            P.dma("scalar", b1e[:], b1_d[e].rearrange("(c p) -> p c", p=128),
                  allow_slow_non_contiguous=True, tag=f"b1 e{e}")
            P.dma("scalar", b2row[:], b2_d[e:e + 1, :], tag=f"b2 e{e}")
            for hf in range(2):
                p_mm(P, PS_m[:, 0:512], crow[:], b2row[0:1, hf * 512:(hf + 1) * 512],
                     tag="b2bc")
                p_copy(P, "vector", b2bc[:, hf * 512:(hf + 1) * 512], PS_m[:, 0:512])
            for q in range(4):
                slot = (seq + 1) % 2
                seq += 1
                if not (g == 1 and e == 7 and q == 3):
                    ng, ne, nq = g, e, q + 1
                    if nq == 4:
                        nq = 0
                        ne += 1
                        if ne == 8:
                            ne = 0
                            ng += 1
                    load_quarter(ne, nq, (seq + 1) % 2)
                for tt in range(4):
                    tok0 = g * 1024 + tt * 256
                    for dc in range(8):
                        bank = PS_h[dc % 2][:, 0:256]
                        for kc in range(8):
                            p_mm(P, bank,
                                 wslot[slot][:, kc * 1024 + dc * 128: kc * 1024 + dc * 128 + 128],
                                 xn_bf[:, kc * TPC + tok0: kc * TPC + tok0 + 256],
                                 start=(kc == 0), stop=(kc == 7), tag="moe1")
                        p_act(P, h_bf[:, dc * 256:(dc + 1) * 256], bank, AF.Relu,
                              bias=b1e[:, (q * 8 + dc):(q * 8 + dc) + 1], tag="hrelu")
                    for s2 in range(2):
                        st8 = (tok0 // 128) + s2
                        mcol = mask_all[:, st8 * 8 + e: st8 * 8 + e + 1]
                        for hf in range(2):
                            bank = PS_y[s2 * 2 + hf][:, 0:512]
                            for kc in range(8):
                                p_mm(P, bank,
                                     h_bf[:, kc * 256 + s2 * 128: kc * 256 + s2 * 128 + 128],
                                     wslot[slot][:, 8192 + kc * 1024 + hf * 512: 8192 + kc * 1024 + hf * 512 + 512],
                                     start=(kc == 0), stop=(kc == 7), tag="moe2")
                            ya = y_acc[:, (tt * 2 + s2) * 1024 + hf * 512:
                                       (tt * 2 + s2) * 1024 + hf * 512 + 512]
                            p_stt(P, ya, bank, mcol, ya, ALU.mult, ALU.add, tag="yflush")
            for st in range(8):
                st8 = g * 8 + st
                mcol = mask_all[:, st8 * 8 + e: st8 * 8 + e + 1]
                for hf in range(2):
                    ya = y_acc[:, st * 1024 + hf * 512: st * 1024 + hf * 512 + 512]
                    p_stt(P, ya, b2bc[:, hf * 512:(hf + 1) * 512], mcol, ya,
                          ALU.mult, ALU.add, tag="yb2")
        if debug and g == 1:
            dbg_ya = nc.dram_tensor("dbg_ya", [128, 8 * 1024], BF16,
                                    kind="ExternalOutput").ap()
            P.dma("sync", dbg_ya, y_acc[:], tag="dbgya", out_final=True)
        for st in range(8):
            st8 = g * 8 + st
            for c in range(8):
                p_trans(P, PS_tb[:, 0:128],
                        y_acc[:, st * 1024 + c * 128: st * 1024 + c * 128 + 128],
                        identb[:], tag="ytr")
                xs = x_sb[:, c * TPC + st8 * 128: c * TPC + st8 * 128 + 128]
                p_tt(P, xs, xs, PS_tb[:, 0:128], ALU.add, tag="resid1")

    for c in range(8):
        P.dma("sync", xo_d[c * 128:(c + 1) * 128, :], x_sb[:, c * TPC:(c + 1) * TPC],
              tag=f"st_x{c}", out_final=True)
    if phases < 4:
        nc2 = P.finalize()
        stack.close()
        return nc2, cp.vals

    P.dma("scalar", wb_sb[:].rearrange("p (kc e) -> p kc e", e=24),
          Wb_d.rearrange("(kc p) e -> p kc e", p=128),
          allow_slow_non_contiguous=True)
    P.dma("scalar", wa_sb2[:].rearrange("p (kc e) -> p kc e", e=8),
          Wa_d.rearrange("(kc p) e -> p kc e", p=128),
          allow_slow_non_contiguous=True)
    ln_phase("ba")
    P.dma("sync", ub_d.rearrange("(n p) c -> p n c", p=128),
          ub_sb[:].rearrange("p (n c) -> p n c", c=24), tag="st_ub", out_final=True)
    P.dma("sync", ua_d.rearrange("(n p) c -> p n c", p=128),
          ua_sb[:].rearrange("p (n c) -> p n c", c=8), tag="st_ua", out_final=True)
    if debug:
        dbg_xn = nc.dram_tensor("dbg_xn", [D, TPC], BF16, kind="ExternalOutput").ap()
        for c in range(8):
            P.dma("sync", dbg_xn[c * 128:(c + 1) * 128, :],
                  xn_bf[:, c * TPC:(c + 1) * TPC], tag="dbgxn", out_final=True)
    if phases < 5:
        nc2 = P.finalize()
        stack.close()
        return nc2, cp.vals

    projs = [(uq_d, Wq_d, 8), (uk_d, Wk_d, 24), (uv_d, Wv_d, 24), (ug_d, Wg_d, 8)]
    pi = 0
    for out_d, W_d, nmc in projs:
        for mc in range(nmc):
            slot = pi % 2
            for kc in range(8):
                P.dma("gpsimd", wslot[slot][:, 15 * 1024 + kc * 128: 15 * 1024 + (kc + 1) * 128],
                      W_d[kc * 128:(kc + 1) * 128, mc * 128:(mc + 1) * 128],
                      tag=f"ldp{pi}")
            for t in range(4):
                bank = PS_y[t % 4][:, 0:512]
                for kc in range(8):
                    p_mm(P, bank,
                         wslot[slot][:, 15 * 1024 + kc * 128: 15 * 1024 + (kc + 1) * 128],
                         xn_bf[:, kc * TPC + t * 512: kc * TPC + t * 512 + 512],
                         start=(kc == 0), stop=(kc == 7), tag="proj")
                ob = obuf[t % 2]
                p_copy(P, "scalar", ob[:], bank)
                P.dma("scalar", out_d[mc * 128:(mc + 1) * 128, t * 512:(t + 1) * 512],
                      ob[:], tag=f"stp{pi}", out_final=True)
            pi += 1

    nc2 = P.finalize()
    stack.close()
    return nc2, cp.vals


NCH_D = 256
CT = 32


def build_launch_D():
    nc = bass.Bass("TRN2", target_bir_lowering=False)
    P = Prog(nc)
    stack = contextlib.ExitStack()
    sb = lambda n, s, d=F32: stack.enter_context(nc.sbuf_tensor(n, s, d))

    TH = T + 3
    uq_d = nc.dram_tensor("uq", [2, 128, TH], BF16, kind="ExternalInput").ap()
    uk_d = nc.dram_tensor("uk", [2, 3, 128, TH], BF16, kind="ExternalInput").ap()
    uv_d = nc.dram_tensor("uv", [2, 3, 128, TH], BF16, kind="ExternalInput").ap()
    ug_d = nc.dram_tensor("ug", [2, 128, T], BF16, kind="ExternalInput").ap()
    ub_d = nc.dram_tensor("ub", [2, 96, NCH_D], F32, kind="ExternalInput").ap()
    ua_d = nc.dram_tensor("ua", [2, 32, NCH_D], F32, kind="ExternalInput").ap()
    cq_w = nc.dram_tensor("cqw", [2, 128, 4], F32, kind="ExternalInput").ap()
    ck_w = nc.dram_tensor("ckw", [2, 3, 128, 4], F32, kind="ExternalInput").ap()
    cv_w = nc.dram_tensor("cvw", [2, 3, 128, 4], F32, kind="ExternalInput").ap()
    alog_d = nc.dram_tensor("alog", [1, 2], F32, kind="ExternalInput").ap()
    dtb_d = nc.dram_tensor("dtb", [1, 2], F32, kind="ExternalInput").ap()
    rmsw_d = nc.dram_tensor("rmsw", [128], F32, kind="ExternalInput").ap()
    cum32_d = nc.dram_tensor("cum32", [32, 32], F32, kind="ExternalInput").ap()
    oh31_d = nc.dram_tensor("oh31", [32, 1], F32, kind="ExternalInput").ap()
    strict_d = nc.dram_tensor("strict", [96, 96], F32, kind="ExternalInput").ap()
    amask_d = nc.dram_tensor("amask", [96, 32], F32, kind="ExternalInput").ap()
    identf_d = nc.dram_tensor("identf", [128, 128], F32, kind="ExternalInput").ap()
    identb_d = nc.dram_tensor("identbd", [128, 128], F32, kind="ExternalInput").ap()
    consts_d = nc.dram_tensor("consts", [128, 8], F32, kind="ExternalInput").ap()
    crow_d = nc.dram_tensor("crow", [1, 128], F32, kind="ExternalInput").ap()
    o_out = nc.dram_tensor("o", [2, 128, T], BF16, kind="ExternalOutput").ap()
    cqflat_d = nc.dram_tensor("cqflat", [1, T // CT * 32], F32, kind="Internal").ap()

    cst = sb("cst", [128, 8])
    crow = sb("crow_sb", [1, 128])
    identb = sb("identb_sb", [128, 128], BF16)
    identf = sb("identf_sb", [128, 128])
    cum32 = sb("cum32_sb", [32, 32])
    oh31 = sb("oh31_sb", [32, 1])
    strict = sb("strict_sb", [96, 96])
    amask = sb("amask_sb", [96, 32])
    rmsw = sb("rmsw_sb", [128, 1])
    cp = ConstPool(nc, cst)

    P.dma("scalar", cst[:], consts_d)
    P.dma("scalar", crow[:], crow_d)
    P.dma("gpsimd", identb[:], identb_d)
    P.dma("scalar", identf[:], identf_d)
    P.dma("scalar", cum32[:], cum32_d)
    P.dma("scalar", oh31[:], oh31_d)
    P.dma("scalar", strict[:], strict_d)
    P.dma("scalar", amask[:], amask_d)
    P.dma("scalar", rmsw[:], rmsw_d.rearrange("(a p) -> p a", p=128),
          allow_slow_non_contiguous=True)

    qc = sb("qc", [128, T], BF16)
    Kc = sb("Kc", [128, 3 * T], BF16)
    Vc = sb("Vc", [128, 3 * T], BF16)
    Kc_v = Kc[:].rearrange("p (c s) -> p c s", s=96)
    Vc_v = Vc[:].rearrange("p (c s) -> p c s", s=96)
    useg = [sb(f"useg{i}", [128, 7 * 1027], BF16) for i in range(2)]
    cw = sb("cw", [128, 7 * 4])
    convt = sb("convt", [128, 1024])
    ug_sb = sb("ug_sb", [128, T], BF16)
    o_acc = sb("o_acc", [128, T], BF16)
    bet = sb("bet", [96, NCH_D])
    nbet = sb("nbet", [96, NCH_D])
    nbb = sb("nbb", [96, NCH_D])
    csub = sb("csub", [96, NCH_D])
    egc = sb("egc", [96, NCH_D])
    g32 = sb("g32", [32, NCH_D])
    egb = sb("egb", [128, NCH_D])
    csubT = sb("csubT", [128, 2 * 96])
    ecqb = sb("ecqb", [128, 512], BF16)
    eg_row = sb("eg_row", [1, NCH_D])
    rr = sb("rr", [1, 512])
    alb = sb("alb", [32, 1])
    dtbb = sb("dtbb", [32, 1])
    spa = sb("spa", [32, NCH_D])
    spb = sb("spb", [32, NCH_D])
    KT = sb("KT", [96, 128], BF16)
    VT = sb("VT", [96, 128], BF16)
    KC2 = sb("KC2", [96, 128], BF16)
    v1 = sb("v1", [96, 128])
    D1 = sb("D1", [96, 96])
    GM = sb("GM", [96, 96])
    Nt = [sb(f"Nt{i}", [96, 96], BF16) for i in range(2)]
    Bt = [sb(f"Bt{i}", [96, 96], BF16) for i in range(2)]
    Xb = [sb(f"Xb{i}", [96, 128], BF16) for i in range(2)]
    Am = sb("Am", [96, 32], BF16)
    Qb = sb("Qb", [128, 32], BF16)
    em = sb("em", [96, 32])
    S_sb = sb("S_sb", [128, 128])
    S_bf = sb("S_bf", [128, 128], BF16)
    t96 = sb("t96", [96, 96])
    r_row = sb("r_row", [1, 512])
    sqo = sb("sqo", [128, 512])
    gsil = sb("gsil", [128, 512], BF16)

    PS = [stack.enter_context(nc.psum_tensor(f"PSd{i}", [128, 512], F32))
          for i in range(6)]
    PSB = [stack.enter_context(nc.psum_tensor(f"PSb{i}", [128, 512], BF16))
           for i in range(2)]

    def bcol(v, parts):
        p_mm(P, PS[0][0:parts, 0:1], crow[0:1, 0:parts], v, tag="bcol")
        return PS[0][0:parts, 0:1]

    al = sb("al_sb", [1, 1])
    dtb = sb("dtb_sb", [1, 1])

    for hd in range(2):
        P.dma("scalar", cw[:, 0:4], cq_w[hd])
        for i in range(3):
            P.dma("scalar", cw[:, 4 + i * 4: 8 + i * 4], ck_w[hd, i])
            P.dma("scalar", cw[:, 16 + i * 4: 20 + i * 4], cv_w[hd, i])
        P.dma("scalar", al[:], alog_d[0:1, hd:hd + 1])
        P.dma("scalar", dtb[:], dtb_d[0:1, hd:hd + 1])
        p_act(P, al[:], al[:], AF.Exp)
        p_ts(P, al[:], al[:], cp.col(-1.0, 1), None, ALU.mult)
        p_copy(P, "vector", alb[:], bcol(al[:], 32))
        p_copy(P, "vector", dtbb[:], bcol(dtb[:], 32))
        P.dma("sync", g32[:], ua_d[hd], tag="ld_ua")
        P.dma("sync", bet[:], ub_d[hd], tag="ld_ub")
        p_act(P, bet[:], bet[:], AF.Sigmoid, scale=1.0)
        p_ts(P, bet[:], bet[:], cp.col(2.0, 96), None, ALU.mult)
        p_ts(P, nbet[:], bet[:], cp.col(-1.0, 96), None, ALU.mult)
        p_ts(P, g32[:], g32[:], dtbb[:, 0:1], None, ALU.add)
        p_act(P, spa[:], g32[:], AF.Abs)
        p_act(P, spa[:], spa[:], AF.Exp, scale=-1.0)
        p_act(P, spa[:], spa[:], AF.Ln, bias=cp.col(1.0, 32))
        p_act(P, spb[:], g32[:], AF.Relu)
        p_tt(P, spb[:], spb[:], spa[:], ALU.add)
        p_ts(P, g32[:], spb[:], alb[:, 0:1], None, ALU.mult)
        p_mm(P, PS[1][0:32, 0:NCH_D], cum32[:], g32[:], tag="cum")
        for r3 in range(3):
            p_copy(P, "vector", csub[32 * r3:32 * r3 + 32, :], PS[1][0:32, 0:NCH_D])
        p_mm(P, PS[2][0:1, 0:NCH_D], oh31[:], csub[0:32, :], tag="gam")
        p_act(P, eg_row[:, 0:NCH_D], PS[2][0:1, 0:NCH_D], AF.Exp)
        p_mm(P, PS[3][:, 0:NCH_D], crow[:], eg_row[:, 0:NCH_D], tag="egb")
        p_copy(P, "vector", egb[:], PS[3][:, 0:NCH_D])
        gam_row = sb(f"gam_row{hd}", [1, NCH_D])
        p_copy(P, "vector", gam_row[:], PS[2][0:1, 0:NCH_D])
        p_mm(P, PS[4][0:96, 0:NCH_D], crow[0:1, 0:96], gam_row[:], tag="gb96")
        p_tt(P, egc[:], PS[4][0:96, 0:NCH_D], csub[:], ALU.subtract)
        p_act(P, egc[:], egc[:], AF.Exp)
        p_act(P, nbb[:], csub[:], AF.Exp)
        p_tt(P, nbb[:], nbb[:], nbet[:], ALU.mult)
        for half in range(2):
            p_trans(P, PS[5][0:128, 0:96],
                    csub[:, half * 128:(half + 1) * 128], identf[0:96, 0:96],
                    tag="csT")
            p_copy(P, "vector", csubT[:, half * 96:(half + 1) * 96], PS[5][0:128, 0:96])
            P.dma("sync",
                  cqflat_d[0:1, half * 4096:(half + 1) * 4096].rearrange(
                      "o (r t) -> (o r) t", t=32),
                  csubT[:, half * 96: half * 96 + 32], tag="st_cqf")
        for seg in range(8):
            us = useg[seg % 2]
            P.dma("sync", us[:, 0:1027], uq_d[hd, :, seg * 1024: seg * 1024 + 1027],
                  tag="ld_uq")
            for i in range(3):
                P.dma("sync", us[:, (1 + i) * 1027:(2 + i) * 1027],
                      uk_d[hd, i, :, seg * 1024: seg * 1024 + 1027], tag="ld_uk")
                P.dma("sync", us[:, (4 + i) * 1027:(5 + i) * 1027],
                      uv_d[hd, i, :, seg * 1024: seg * 1024 + 1027], tag="ld_uv")
            for pl in range(7):
                base = pl * 1027
                eng = "vector"
                p_ts(P, convt[:], us[:, base:base + 1024], cw[:, pl * 4:pl * 4 + 1],
                     None, ALU.mult, eng=eng, tag="cv0")
                for i in range(1, 4):
                    p_stt(P, convt[:], us[:, base + i:base + i + 1024],
                          cw[:, pl * 4 + i:pl * 4 + i + 1], convt[:],
                          ALU.mult, ALU.add, eng=eng, tag=f"cv{i}")
                if pl == 0:
                    dst = qc[:, seg * 1024:(seg + 1) * 1024]
                elif pl <= 3:
                    i = pl - 1
                    dst = Kc_v[:, 32 * seg:32 * (seg + 1), 32 * i:32 * (i + 1)]
                else:
                    i = pl - 4
                    dst = Vc_v[:, 32 * seg:32 * (seg + 1), 32 * i:32 * (i + 1)]
                p_act(P, dst, convt[:], AF.Silu, tag="silu")
        def l2n(dst_slices, extra_scale):
            for dsl in dst_slices:
                p_act(P, sqo[:], dsl, AF.Square)
                p_mm(P, PS[5][0:1, 0:512], cp.col(1.0), sqo[:], tag="l2mm")
                p_act(P, r_row[:], PS[5][0:1, 0:512], AF.Ln, bias=cp.col(1e-6, 1))
                p_act(P, r_row[:], r_row[:], AF.Exp, scale=-0.5)
                if extra_scale != 1.0:
                    p_ts(P, r_row[:], r_row[:], cp.col(extra_scale, 1), None, ALU.mult)
                p_mm(P, PS[4][:, 0:512], crow[:], r_row[:], tag="l2bc")
                p_tt(P, dsl, dsl, PS[4][:, 0:512], ALU.mult, tag="l2ap")
        l2n([qc[:, i * 512:(i + 1) * 512] for i in range(16)], float(128.0 ** -0.5))
        l2n([Kc[:, i * 512:(i + 1) * 512] for i in range(48)], 1.0)
        P.dma("sync", ug_sb[:], ug_d[hd], tag="ld_ug")
        p_memset(P, "vector", S_sb[:], 0.0)
        p_memset(P, "vector", S_bf[:], 0.0)

        for c in range(NCH_D):
            gq = c // 16
            if c % 16 == 0:
                P.dma("sync", rr[0:1, 0:512], cqflat_d[0:1, gq * 512:(gq + 1) * 512],
                      tag="ld_cqf")
                p_act(P, rr[0:1, 0:512], rr[0:1, 0:512], AF.Exp)
                p_mm(P, PS[3][:, 0:512], crow[:], rr[0:1, 0:512], tag="ecqb")
                p_copy(P, "scalar", ecqb[:], PS[3][:, 0:512])
            Kch = Kc[:, c * 96:(c + 1) * 96]
            Vch = Vc[:, c * 96:(c + 1) * 96]
            Qch = qc[:, c * CT:(c + 1) * CT]
            ccol = csub[:, c:c + 1]
            p_mm(P, PS[0][0:1, 0:96], identf[:, (c % 128):(c % 128) + 1],
                 csubT[:, (c // 128) * 96:(c // 128) * 96 + 96], tag="sel")
            p_copy(P, "vector", rr[0:1, 0:96], PS[0][0:1, 0:96])
            p_mm(P, PS[0][0:96, 0:96], crow[0:1, 0:96], rr[0:1, 0:96], tag="bc")
            p_ts(P, t96[:], PS[0][0:96, 0:96], ccol, None, ALU.subtract, tag="d1a")
            p_act(P, t96[:], t96[:], AF.Relu)
            p_act(P, D1[:], t96[:], AF.Exp, scale=-1.0)
            p_mm(P, PS[1][0:96, 0:96], Kch, Kch, tag="G")
            p_tt(P, GM[:], PS[1][0:96, 0:96], strict[:], ALU.mult, tag="gm")
            p_stt(P, Nt[0][:], GM[:], nbet[:, c:c + 1], D1[:], ALU.mult, ALU.mult,
                  tag="N")
            p_trans(P, PSB[0][0:96, 0:96], Nt[0][:], identb[0:96, 0:96], tag="trN")
            p_copy(P, "vector", Bt[0][:], PSB[0][0:96, 0:96])
            p_trans(P, PSB[1][0:96, 0:128], Kch, identb[:], tag="trK")
            p_copy(P, "vector", KT[:], PSB[1][0:96, 0:128])
            p_trans(P, PSB[0][0:96, 128:256], Vch, identb[:], tag="trV")
            p_copy(P, "vector", VT[:], PSB[0][0:96, 128:256])
            p_mm(P, PS[4][0:96, 0:128], Kch, S_bf[:], tag="KS0")
            p_ts(P, v1[:], VT[:], bet[:, c:c + 1], None, ALU.mult, tag="bv")
            p_stt(P, Xb[0][:], PS[4][0:96, 0:128], nbb[:, c:c + 1], v1[:],
                  ALU.mult, ALU.add, tag="B2")
            p_ts(P, KC2[:], KT[:], egc[:, c:c + 1], None, ALU.mult, tag="kc2")
            for k in range(7):
                xcur = Xb[k % 2]
                xnxt = Xb[(k + 1) % 2]
                p_mm(P, PS[2][0:96, 0:128], identb[0:96, 0:96], xcur[:],
                     start=True, stop=False, tag="apI")
                p_mm(P, PS[2][0:96, 0:128], Bt[k % 2][:], xcur[:],
                     start=False, stop=True, tag="apN")
                p_copy(P, "vector" if k % 2 == 0 else "scalar", xnxt[:],
                       PS[2][0:96, 0:128], tag="apE")
                if k < 6:
                    a, b_ = Nt[k % 2], Bt[k % 2]
                    p_mm(P, PS[3][0:96, 0:96], b_[:], a[:], tag="sqA")
                    p_copy(P, "scalar", Nt[(k + 1) % 2][:], PS[3][0:96, 0:96],
                           tag="sqAe")
                    p_mm(P, PS[5][0:96, 0:96], a[:], b_[:], tag="sqB")
                    p_copy(P, "vector", Bt[(k + 1) % 2][:], PS[5][0:96, 0:96],
                           tag="sqBe")
            Z = Xb[7 % 2]
            p_mm(P, PS[1][0:96, 0:32], Kch, Qch, tag="AtQ")
            p_mm(P, PS[0][0:96, 0:32], crow[0:1, 0:96], rr[0:1, 0:32], tag="bc2")
            p_ts(P, em[:], PS[0][0:96, 0:32], ccol, None, ALU.subtract, tag="ama")
            p_act(P, em[:], em[:], AF.Relu, scale=-1.0)
            p_act(P, em[:], em[:], AF.Exp, scale=-1.0)
            p_tt(P, em[:], em[:], amask[:], ALU.mult, tag="amm")
            p_tt(P, Am[:], PS[1][0:96, 0:32], em[:], ALU.mult, tag="amf")
            p_tt(P, Qb[:], Qch, ecqb[:, (c % 16) * 32:(c % 16) * 32 + 32], ALU.mult,
                 tag="qb")
            p_mm(P, PS[4][0:128, 128:160], Z[:], Am[:], start=True, stop=False,
                 tag="oZ")
            p_mm(P, PS[4][0:128, 128:160], S_bf[:], Qb[:], start=False, stop=True,
                 tag="oS")
            p_copy(P, "scalar", o_acc[:, c * CT:(c + 1) * CT], PS[4][0:128, 128:160],
                   tag="oev")
            p_mm(P, PS[4][0:128, 160:288], KC2[:], Z[:], tag="Supd")
            p_stt(P, S_sb[:], S_sb[:], egb[:, c:c + 1], PS[4][0:128, 160:288],
                  ALU.mult, ALU.add, tag="Sacc")
            p_copy(P, "scalar", S_bf[:], S_sb[:], tag="Sbf")
            if c % 16 == 15:
                osl = o_acc[:, gq * 512:(gq + 1) * 512]
                p_act(P, sqo[:], osl, AF.Square)
                p_mm(P, PS[5][0:1, 0:512], cp.col(1.0), sqo[:], tag="rmsmm")
                p_act(P, r_row[:], PS[5][0:1, 0:512], AF.Ln, scale=1.0 / 128,
                      bias=cp.col(1e-5, 1))
                p_act(P, r_row[:], r_row[:], AF.Exp, scale=-0.5)
                p_mm(P, PS[3][:, 0:512], crow[:], r_row[:], tag="rmsbc")
                p_act(P, gsil[:], ug_sb[:, gq * 512:(gq + 1) * 512], AF.Silu)
                p_tt(P, sqo[:], osl, PS[3][:, 0:512], ALU.mult, tag="orms")
                p_ts(P, sqo[:], sqo[:], rmsw[:, 0:1], None, ALU.mult, tag="ormw")
                p_tt(P, osl, sqo[:], gsil[:], ALU.mult, tag="ogate")
        P.dma("sync", o_out[hd], o_acc[:], tag=f"st_o{hd}", out_final=True)

    nc2 = P.finalize()
    stack.close()
    return nc2, cp.vals


def build_launch_E():
    nc = bass.Bass("TRN2", target_bir_lowering=False)
    P = Prog(nc)
    stack = contextlib.ExitStack()
    sb = lambda n, s, d=F32: stack.enter_context(nc.sbuf_tensor(n, s, d))
    psf = lambda n: stack.enter_context(nc.psum_tensor(n, [128, 512], F32))

    xT_d = nc.dram_tensor("xT", [D, TPC], F32, kind="ExternalInput").ap()
    oT_d = nc.dram_tensor("oT", [D, TPC], BF16, kind="ExternalInput").ap()
    Wo_d = nc.dram_tensor("Wo", [D, D], F32, kind="ExternalInput").ap()
    W1_d = nc.dram_tensor("W1", [4096, D], F32, kind="ExternalInput").ap()
    b1_d = nc.dram_tensor("b1", [4096], F32, kind="ExternalInput").ap()
    W2_d = nc.dram_tensor("W2", [D, 4096], F32, kind="ExternalInput").ap()
    b2_d = nc.dram_tensor("b2", [D], F32, kind="ExternalInput").ap()
    ident_d = nc.dram_tensor("ident", [128, 128], F32, kind="ExternalInput").ap()
    consts_d = nc.dram_tensor("consts", [128, 8], F32, kind="ExternalInput").ap()
    crow_d = nc.dram_tensor("crow", [1, 128], F32, kind="ExternalInput").ap()
    xo_d = nc.dram_tensor("x_out", [D, TPC], F32, kind="ExternalOutput").ap()

    x_sb = sb("x_sb", [128, 8 * TPC])
    xn_bf = sb("xn_bf", [128, 8 * TPC], BF16)
    wslot = [sb(f"wslot{i}", [128, 2048]) for i in range(2)]
    w1T = sb("w1T", [128, 4 * 8 * 128], BF16)
    w2T = sb("w2T", [128, 4 * 1024], BF16)
    tern = sb("tern", [128, 1024], BF16)
    ta = sb("ta", [128, 1024])
    tb = sb("tb", [128, 1024])
    h_bf = sb("h_bf", [128, 4 * 256], BF16)
    y_acc = sb("y_acc", [128, 16 * 1024], BF16)
    b1e = sb("b1e", [128, 32])
    b2e = sb("b2e", [128, 8])
    cst = sb("cst", [128, 8])
    crow = sb("crow_sb", [1, 128])
    identb = sb("identb_sb", [128, 128], BF16)
    acc_abs = sb("acc_abs", [128, 2])
    thr = sb("thr", [1, 2])
    thrb = sb("thrb", [128, 4])
    W = 256
    scr = dict(sq0=sb("sq0", [128, W]), sq1=sb("sq1", [128, W]),
               r_m=sb("r_m", [1, W]), r_ss=sb("r_ss", [1, W]),
               r_a=sb("r_a", [1, W]), r_b=sb("r_b", [1, W]),
               r_t=sb("r_t", [1, W]),
               A2=sb("A2", [128, W]), B2=sb("B2", [128, W]))

    PS_y = [psf(f"PS_y{i}") for i in range(4)]
    PS_h = [psf(f"PS_h{i}") for i in range(2)]
    PS_m = psf("PS_m")
    PS_tb = stack.enter_context(nc.psum_tensor("PS_tb", [128, 512], BF16))

    cp = ConstPool(nc, cst)
    b1cols = feat_rows(nc, P, "b1cols", b1_d, 4096, stack)
    b2cols = feat_rows(nc, P, "b2cols", b2_d, D, stack)
    P.dma("scalar", cst[:], consts_d)
    P.dma("scalar", crow[:], crow_d)
    P.dma("gpsimd", identb[:], ident_d)
    for c in range(8):
        P.dma("sync", x_sb[:, c * TPC:(c + 1) * TPC], xT_d[c * 128:(c + 1) * 128, :])

    wo_bf = sb("wo_bf", [128, 8192], BF16)
    for kc in range(8):
        P.dma("gpsimd", wo_bf[:, kc * 1024:(kc + 1) * 1024],
              Wo_d[kc * 128:(kc + 1) * 128, :], tag=f"ld_wo{kc}")
    o_sb = xn_bf
    for c in range(8):
        P.dma("sync", o_sb[:, c * TPC:(c + 1) * TPC], oT_d[c * 128:(c + 1) * 128, :])
    for t in range(4):
        for mc in range(8):
            bank = PS_y[mc % 4][:, 0:512]
            for kc in range(8):
                p_mm(P, bank, wo_bf[:, kc * 1024 + mc * 128: kc * 1024 + mc * 128 + 128],
                     o_sb[:, kc * TPC + t * 512: kc * TPC + t * 512 + 512],
                     start=(kc == 0), stop=(kc == 7))
            xs = x_sb[:, mc * TPC + t * 512: mc * TPC + t * 512 + 512]
            p_tt(P, xs, xs, bank, ALU.add, tag="resid0")

    for t8 in range(8):
        xc = [x_sb[:, c * TPC + t8 * 256: c * TPC + t8 * 256 + 256] for c in range(8)]
        scr["PS_sum"] = PS_h[0][0:1, 0:256]
        scr["PS_sq"] = PS_h[1][0:1, 0:256]
        scr["PS_A"] = PS_y[0][:, 0:256]
        scr["PS_B"] = PS_y[1][:, 0:256]
        emit_stats(P, cp, crow, xc, D, 0, scr, 1e-5)
        for c in range(8):
            p_tt(P, scr["sq0"][:], xc[c], scr["A2"][:], ALU.mult)
            p_tt(P, scr["sq1"][:], scr["sq0"][:], scr["B2"][:], ALU.add)
            p_copy(P, "scalar", xn_bf[:, c * TPC + t8 * 256: c * TPC + t8 * 256 + 256],
                   scr["sq1"][:])

    p_memset(P, "vector", acc_abs[:], 0.0)
    for blk_i in range(32):
        slot = wslot[blk_i % 2][:, 0:1024]
        P.dma("sync", slot, W1_d[blk_i * 128:(blk_i + 1) * 128, :], tag=f"a1_{blk_i}")
        p_red(P, scr["sq0"][:, 0:1], slot, ALU.add, absv=True, tag="absr")
        p_tt(P, acc_abs[:, 0:1], acc_abs[:, 0:1], scr["sq0"][:, 0:1], ALU.add)
    for blk_i in range(32):
        slot = wslot[blk_i % 2][:, 0:1024]
        P.dma("sync", slot, W2_d[(blk_i // 4) * 128:(blk_i // 4 + 1) * 128,
                                 (blk_i % 4) * 1024:(blk_i % 4 + 1) * 1024],
              tag=f"a2_{blk_i}")
        p_red(P, scr["sq0"][:, 1:2], slot, ALU.add, absv=True, tag="absr2")
        p_tt(P, acc_abs[:, 1:2], acc_abs[:, 1:2], scr["sq0"][:, 1:2], ALU.add)
    p_mm(P, PS_m[0:1, 0:2], cp.col(1.0), acc_abs[:], tag="abssum")
    p_act(P, thr[:], PS_m[0:1, 0:2], AF.Relu, scale=1.0 / (4096 * 1024),
          bias=cp.col(-1e-8, 1))
    p_ts(P, thr[:], thr[:], cp.col(1e-8, 1), None, ALU.add)
    p_ts(P, thr[:], thr[:], cp.col(0.5, 1), None, ALU.mult)
    p_mm(P, PS_m[:, 4:6], crow[:], thr[:], tag="thrbc")
    p_copy(P, "vector", thrb[:, 0:1], PS_m[:, 4:5])
    p_copy(P, "vector", thrb[:, 2:3], PS_m[:, 5:6])
    p_ts(P, thrb[:, 1:2], thrb[:, 0:1], cp.col(-1.0), None, ALU.mult)
    p_ts(P, thrb[:, 3:4], thrb[:, 2:3], cp.col(-1.0), None, ALU.mult)

    def tern_tiles(raw, tcol, ncols, out_tiles, psb):
        p_ts(P, ta[:, 0:ncols], raw, thrb[:, tcol:tcol + 1], None, ALU.is_gt,
             tag="tga")
        p_ts(P, tb[:, 0:ncols], raw, thrb[:, tcol + 1:tcol + 2], None, ALU.is_lt,
             tag="tgb")
        p_tt(P, tern[:, 0:ncols], ta[:, 0:ncols], tb[:, 0:ncols], ALU.subtract,
             tag="tsub")
        for j in range(ncols // 128):
            p_trans(P, psb[0:128, 0:128], tern[:, j * 128:(j + 1) * 128], identb[:],
                    tag="ttr")
            p_copy(P, "vector", out_tiles[j], psb[0:128, 0:128], tag="tte")

    p_memset(P, "vector", y_acc[:], 0.0)
    for q in range(8):
        for rb in range(4):
            slot = wslot[rb % 2][:, 0:1024]
            P.dma("sync", slot, W1_d[q * 512 + rb * 128: q * 512 + (rb + 1) * 128, :],
                  tag=f"w1 q{q}r{rb}")
            tern_tiles(slot, 0, 1024,
                       [w1T[:, kc * 512 + rb * 128: kc * 512 + (rb + 1) * 128]
                        for kc in range(8)], PS_tb)
        for rb in range(8):
            slot = wslot[rb % 2][:, 1024:1536]
            P.dma("sync", slot, W2_d[rb * 128:(rb + 1) * 128, q * 512:(q + 1) * 512],
                  tag=f"w2 q{q}r{rb}")
            tern_tiles(slot, 2, 512,
                       [w2T[:, ffc * 1024 + rb * 128: ffc * 1024 + (rb + 1) * 128]
                        for ffc in range(4)], PS_tb)
        for tt in range(8):
            tok0 = tt * 256
            for dc in range(4):
                bank = PS_h[dc % 2][:, 0:256]
                for kc in range(8):
                    p_mm(P, bank, w1T[:, kc * 512 + dc * 128: kc * 512 + dc * 128 + 128],
                         xn_bf[:, kc * TPC + tok0: kc * TPC + tok0 + 256],
                         start=(kc == 0), stop=(kc == 7), tag="ffn1")
                p_act(P, h_bf[:, dc * 256:(dc + 1) * 256], bank, AF.Gelu,
                      bias=b1cols[:, (q * 4 + dc):(q * 4 + dc) + 1], tag="hgelu")
            for s2 in range(2):
                st8 = tt * 2 + s2
                for hf in range(2):
                    bank = PS_y[s2 * 2 + hf][:, 0:512]
                    for kc in range(4):
                        p_mm(P, bank,
                             h_bf[:, kc * 256 + s2 * 128: kc * 256 + s2 * 128 + 128],
                             w2T[:, kc * 1024 + hf * 512: kc * 1024 + hf * 512 + 512],
                             start=(kc == 0), stop=(kc == 3), tag="ffn2")
                    ya = y_acc[:, st8 * 1024 + hf * 512: st8 * 1024 + hf * 512 + 512]
                    p_tt(P, ya, ya, bank, ALU.add, tag="yflush")

    for st8 in range(16):
        for c in range(8):
            p_trans(P, PS_tb[:, 0:128],
                    y_acc[:, st8 * 1024 + c * 128: st8 * 1024 + c * 128 + 128],
                    identb[:], tag="ytr")
            xs = x_sb[:, c * TPC + st8 * 128: c * TPC + st8 * 128 + 128]
            p_stt(P, xs, PS_tb[:, 0:128], b2cols[:, c:c + 1], xs, ALU.add, ALU.add,
                  tag="resid1")
    for c in range(8):
        P.dma("sync", xo_d[c * 128:(c + 1) * 128, :], x_sb[:, c * TPC:(c + 1) * TPC],
              tag=f"st_x{c}", out_final=True)
    nc2 = P.finalize()
    stack.close()
    return nc2, cp.vals


_PROGS = {}


def _prog(name):
    if name not in _PROGS:
        import os as _os
        _PROGS[name] = {"A": build_launch_A, "B": build_launch_B,
                        "C": build_launch_C, "D": build_launch_D,
                        "E": build_launch_E,
                        "F": lambda: build_fused(
                            int(_os.environ.get("FPHASES", "9")))}[name]()
    return _PROGS[name]


def _consts_arr(cvals):
    return np.tile(np.array(list(cvals) + [0.0] * (8 - len(cvals)), np.float32),
                   (128, 1))


def kernel_unfused(**inp):
    f32 = np.float32
    ca = np.ascontiguousarray
    x0 = np.asarray(inp["L0_attn_x"], f32)
    _ = (inp["L1_attn_cos"], inp["L1_attn_sin"])
    crow = np.ones((1, 128), f32)
    ident = np.eye(128, dtype=f32)

    ncA, cvA = _prog("A")
    in_maps = []
    for c in range(NCORE):
        b, s = c // 4, c % 4
        in_maps.append(dict(
            xT=ca(x0[b, s * TPC:(s + 1) * TPC].T),
            Wqkv=np.asarray(inp["L0_attn_Wqkv"], f32),
            bqkv=np.asarray(inp["L0_attn_bqkv"], f32),
            gq=np.asarray(inp["L0_attn_gamma_q"], f32),
            bq=np.asarray(inp["L0_attn_beta_q"], f32),
            gk=np.asarray(inp["L0_attn_gamma_k"], f32),
            bk=np.asarray(inp["L0_attn_beta_k"], f32),
            consts=_consts_arr(cvA), crow=crow))
    rA = run_bass_kernel_spmd(ncA, in_maps, core_ids=list(range(NCORE)))
    kerq = [np.concatenate([np.asarray(rA.results[4 * b + s]["kerq"]) for s in range(4)], 1)
            for b in range(2)]
    kerk = [np.concatenate([np.asarray(rA.results[4 * b + s]["kerk"]) for s in range(4)], 1)
            for b in range(2)]
    vv = [np.concatenate([np.asarray(rA.results[4 * b + s]["v"]) for s in range(4)], 1)
          for b in range(2)]

    ncB, cvB = _prog("B")
    cmask = np.tril(np.ones((128, 128), f32)).T.copy()
    onesb = np.ones((128, 1), ml_dtypes.bfloat16)
    in_maps = []
    for c in range(NCORE):
        hh = [4 * c + i for i in range(4)]
        def hp(arr_list, g0, g1, tr=False):
            b = g0 // 16
            r0, r1 = (g0 % 16) * 64, (g1 % 16) * 64 + 64
            m = arr_list[b][r0:r1, :]
            return ca(m.T) if tr else ca(m)
        kq = np.stack([hp(kerq, hh[2 * p], hh[2 * p] + 1) for p in range(2)])
        kk = np.stack([hp(kerk, hh[2 * p], hh[2 * p] + 1) for p in range(2)])
        kkT = np.stack([hp(kerk, hh[2 * p], hh[2 * p] + 1, tr=True) for p in range(2)])
        vT = np.stack([hp(vv, hh[2 * p], hh[2 * p] + 1, tr=True) for p in range(2)])
        in_maps.append(dict(kq=kq, kk=kk, kkT=kkT, vT=vT, cmask=cmask,
                            consts=_consts_arr(cvB), crow=crow, onesb=onesb))
    rB = run_bass_kernel_spmd(ncB, in_maps, core_ids=list(range(NCORE)))
    o0 = []
    for b in range(2):
        rows = []
        for h in range(16):
            core, p = (4 * b + h // 4), (h % 4) // 2
            rows.append(np.asarray(rB.results[core]["o"])[p, 64 * (h % 2 + 2 * 0):, :]
                        if False else
                        np.asarray(rB.results[core]["o"])[(h % 4) // 2,
                                                          64 * (h % 2):64 * (h % 2) + 64, :])
        o0.append(np.concatenate(rows, 0))

    ncC, cvC = _prog("C")
    in_maps = []
    for c in range(NCORE):
        b, s = c // 4, c % 4
        in_maps.append(dict(
            xT=ca(x0[b, s * TPC:(s + 1) * TPC].T),
            oT=ca(o0[b][:, s * TPC:(s + 1) * TPC]),
            Wo=np.asarray(inp["L0_attn_Wo"], f32), bo=np.asarray(inp["L0_attn_bo"], f32),
            gwT=ca(np.asarray(inp["L0_ffn_gate_w"], f32).T),
            gb=np.asarray(inp["L0_ffn_gate_b"], f32).reshape(1, 8),
            eb=np.asarray(inp["L0_ffn_expert_bias"], f32).reshape(1, 8),
            w1=np.asarray(inp["L0_ffn_w1"], f32), b1=np.asarray(inp["L0_ffn_b1"], f32),
            w2=np.asarray(inp["L0_ffn_w2"], f32), b2=np.asarray(inp["L0_ffn_b2"], f32),
            Wq=np.asarray(inp["L1_attn_Wq"], f32), Wk=np.asarray(inp["L1_attn_Wk"], f32),
            Wv=np.asarray(inp["L1_attn_Wv"], f32), Wb=np.asarray(inp["L1_attn_Wb"], f32),
            Wa=np.asarray(inp["L1_attn_Wa"], f32), Wg=np.asarray(inp["L1_attn_Wg"], f32),
            ident=ident, consts=_consts_arr(cvC), crow=crow))
    rC = run_bass_kernel_spmd(ncC, in_maps, core_ids=list(range(NCORE)))
    x1c = [np.asarray(rC.results[c]["x_out"]) for c in range(NCORE)]
    cat = lambda key, b: np.concatenate(
        [np.asarray(rC.results[4 * b + s][key]) for s in range(4)], 1)
    uq = [cat("uq", b) for b in range(2)]
    uk = [cat("uk", b) for b in range(2)]
    uv = [cat("uv", b) for b in range(2)]
    ug = [cat("ug", b) for b in range(2)]
    ubf = [np.concatenate([np.asarray(rC.results[4 * b + s]["ub"]) for s in range(4)], 0)
           for b in range(2)]
    uaf = [np.concatenate([np.asarray(rC.results[4 * b + s]["ua"]) for s in range(4)], 0)
           for b in range(2)]

    ncD, cvD = _prog("D")
    cum32 = (np.arange(32)[:, None] <= np.arange(32)[None, :]).astype(f32)
    oh31 = np.zeros((32, 1), f32); oh31[31, 0] = 1.0
    order = (np.arange(96) % 32) * 3 + (np.arange(96) // 32)
    strict = (order[:, None] > order[None, :]).astype(f32)
    amask = ((np.arange(96)[:, None] % 32) <= np.arange(32)[None, :]).astype(f32)
    conv_q = np.asarray(inp["L1_attn_conv_q"], f32)
    conv_k = np.asarray(inp["L1_attn_conv_k"], f32)
    conv_v = np.asarray(inp["L1_attn_conv_v"], f32)
    A_log = np.asarray(inp["L1_attn_A_log"], f32)
    dt_bias = np.asarray(inp["L1_attn_dt_bias"], f32)
    zpad = np.zeros((128, 3), ml_dtypes.bfloat16)
    in_maps = []
    for c in range(NCORE):
        heads = [2 * c, 2 * c + 1]
        uq_l, uk_l, uv_l, ug_l, ub_l, ua_l = [], [], [], [], [], []
        cqw_l, ckw_l, cvw_l = [], [], []
        for g in heads:
            b, hh = g // 8, g % 8
            uq_l.append(np.concatenate([zpad, uq[b][hh * 128:(hh + 1) * 128]], 1))
            uk_l.append(np.stack([np.concatenate(
                [zpad, uk[b][(i * 8 + hh) * 128:(i * 8 + hh + 1) * 128]], 1)
                for i in range(3)]))
            uv_l.append(np.stack([np.concatenate(
                [zpad, uv[b][(i * 8 + hh) * 128:(i * 8 + hh + 1) * 128]], 1)
                for i in range(3)]))
            ug_l.append(ug[b][hh * 128:(hh + 1) * 128])
            ubh = ubf[b][:, [i * 8 + hh for i in range(3)]]
            ub_l.append(np.transpose(ubh.reshape(256, 32, 3), (2, 1, 0)).reshape(96, 256))
            ua_l.append(ca(uaf[b][:, hh].reshape(256, 32).T))
            cqw_l.append(conv_q[hh * 128:(hh + 1) * 128])
            ckw_l.append(np.stack([conv_k[(i * 8 + hh) * 128:(i * 8 + hh + 1) * 128]
                                   for i in range(3)]))
            cvw_l.append(np.stack([conv_v[(i * 8 + hh) * 128:(i * 8 + hh + 1) * 128]
                                   for i in range(3)]))
        in_maps.append(dict(
            uq=np.stack(uq_l), uk=np.stack(uk_l), uv=np.stack(uv_l),
            ug=np.stack(ug_l), ub=ca(np.stack(ub_l)), ua=np.stack(ua_l),
            cqw=np.stack(cqw_l), ckw=np.stack(ckw_l), cvw=np.stack(cvw_l),
            alog=A_log[[g % 8 for g in heads]].reshape(1, 2),
            dtb=dt_bias[[g % 8 for g in heads]].reshape(1, 2),
            rmsw=np.asarray(inp["L1_attn_rms_weight"], f32),
            cum32=cum32, oh31=oh31, strict=strict, amask=amask,
            identf=ident, identbd=ident, consts=_consts_arr(cvD), crow=crow))
    rD = run_bass_kernel_spmd(ncD, in_maps, core_ids=list(range(NCORE)))
    og = []
    for b in range(2):
        rows = [np.asarray(rD.results[(b * 8 + hh) // 2]["o"])[(b * 8 + hh) % 2]
                for hh in range(8)]
        og.append(np.concatenate(rows, 0))

    ncE, cvE = _prog("E")
    in_maps = []
    for c in range(NCORE):
        b, s = c // 4, c % 4
        in_maps.append(dict(
            xT=x1c[c], oT=ca(og[b][:, s * TPC:(s + 1) * TPC]),
            Wo=np.asarray(inp["L1_attn_Wo"], f32),
            W1=np.asarray(inp["L1_ffn_W1"], f32), b1=np.asarray(inp["L1_ffn_b1"], f32),
            W2=np.asarray(inp["L1_ffn_W2"], f32), b2=np.asarray(inp["L1_ffn_b2"], f32),
            ident=ident, consts=_consts_arr(cvE), crow=crow))
    rE = run_bass_kernel_spmd(ncE, in_maps, core_ids=list(range(NCORE)))
    out = np.empty((2, T, D), f32)
    for c in range(NCORE):
        b, s = c // 4, c % 4
        out[b, s * TPC:(s + 1) * TPC, :] = np.asarray(rE.results[c]["x_out"]).T
    return out


def p_cc(P, kind, op, ins, outs, tag="cc"):
    def emit(e, k=kind, o=op, i=ins, u=outs):
        return e.collective_compute(k, o, replica_groups=[list(range(8))],
                                    ins=[i.ap().opt()], outs=[u.ap().opt()])
    return P.op("gpsimd", emit, reads=[ins.ap()], writes=[outs.ap()], tag=tag)


def build_fused(phases=9):
    nc = bass.Bass("TRN2", target_bir_lowering=False)
    P = Prog(nc)
    stack = contextlib.ExitStack()
    sb = lambda n, s, d=F32: stack.enter_context(nc.sbuf_tensor(n, s, d))
    psf = lambda n: stack.enter_context(nc.psum_tensor(n, [128, 512], F32))
    NCHD = 64

    xT_d = nc.dram_tensor("xT", [D, TPC], F32, kind="ExternalInput").ap()
    Wqkv_d = nc.dram_tensor("Wqkv", [D, 3 * D], F32, kind="ExternalInput").ap()
    bqkv_d = nc.dram_tensor("bqkv", [3 * D], F32, kind="ExternalInput").ap()
    gq_d = nc.dram_tensor("gq", [D], F32, kind="ExternalInput").ap()
    bq_d = nc.dram_tensor("bq", [D], F32, kind="ExternalInput").ap()
    gk_d = nc.dram_tensor("gk", [D], F32, kind="ExternalInput").ap()
    bk_d = nc.dram_tensor("bk", [D], F32, kind="ExternalInput").ap()
    Wo0_d = nc.dram_tensor("Wo0", [D, D], F32, kind="ExternalInput").ap()
    bo_d = nc.dram_tensor("bo", [D], F32, kind="ExternalInput").ap()
    gwT_d = nc.dram_tensor("gwT", [D, 8], F32, kind="ExternalInput").ap()
    gb_d = nc.dram_tensor("gb", [1, 8], F32, kind="ExternalInput").ap()
    eb_d = nc.dram_tensor("eb", [1, 8], F32, kind="ExternalInput").ap()
    w1_d = nc.dram_tensor("w1", [8, D, 4096], F32, kind="ExternalInput").ap()
    b1_d = nc.dram_tensor("b1", [8, 4096], F32, kind="ExternalInput").ap()
    w2_d = nc.dram_tensor("w2", [8, 4096, D], F32, kind="ExternalInput").ap()
    b2_d = nc.dram_tensor("b2", [8, D], F32, kind="ExternalInput").ap()
    Wq_d = nc.dram_tensor("Wq", [D, D], F32, kind="ExternalInput").ap()
    Wk_d = nc.dram_tensor("Wk", [D, 3 * D], F32, kind="ExternalInput").ap()
    Wv_d = nc.dram_tensor("Wv", [D, 3 * D], F32, kind="ExternalInput").ap()
    Wb_d = nc.dram_tensor("Wb", [D, 24], F32, kind="ExternalInput").ap()
    Wa_d = nc.dram_tensor("Wa", [D, 8], F32, kind="ExternalInput").ap()
    Wg_d = nc.dram_tensor("Wg", [D, D], F32, kind="ExternalInput").ap()
    cq_w = nc.dram_tensor("cqw", [8, 128, 4], F32, kind="ExternalInput").ap()
    ck_w = nc.dram_tensor("ckw", [8, 3, 128, 4], F32, kind="ExternalInput").ap()
    cv_w = nc.dram_tensor("cvw", [8, 3, 128, 4], F32, kind="ExternalInput").ap()
    alog_d = nc.dram_tensor("alog", [1, 8], F32, kind="ExternalInput").ap()
    dtb_d = nc.dram_tensor("dtb", [1, 8], F32, kind="ExternalInput").ap()
    rmsw_d = nc.dram_tensor("rmsw", [128], F32, kind="ExternalInput").ap()
    Wo1_d = nc.dram_tensor("Wo1", [D, D], F32, kind="ExternalInput").ap()
    W1f_d = nc.dram_tensor("W1f", [4096, D], F32, kind="ExternalInput").ap()
    b1f_d = nc.dram_tensor("b1f", [4096], F32, kind="ExternalInput").ap()
    W2f_d = nc.dram_tensor("W2f", [D, 4096], F32, kind="ExternalInput").ap()
    b2f_d = nc.dram_tensor("b2f", [D], F32, kind="ExternalInput").ap()
    ident_d = nc.dram_tensor("ident", [128, 128], F32, kind="ExternalInput").ap()
    consts_d = nc.dram_tensor("consts", [128, 16], F32, kind="ExternalInput").ap()
    crow_d = nc.dram_tensor("crow", [1, 128], F32, kind="ExternalInput").ap()
    cmask_d = nc.dram_tensor("cmask", [128, 128], F32, kind="ExternalInput").ap()
    cum32_d = nc.dram_tensor("cum32", [32, 32], F32, kind="ExternalInput").ap()
    oh31_d = nc.dram_tensor("oh31", [32, 1], F32, kind="ExternalInput").ap()
    strict_d = nc.dram_tensor("strict", [96, 96], F32, kind="ExternalInput").ap()
    amask_d = nc.dram_tensor("amask", [96, 32], F32, kind="ExternalInput").ap()
    pmask_d = nc.dram_tensor("pmask", [1, 8], F32, kind="ExternalInput").ap()
    hmask_d = nc.dram_tensor("hmask", [1, 8], F32, kind="ExternalInput").ap()
    onesb_d = nc.dram_tensor("onesb", [128, 1], BF16, kind="ExternalInput").ap()
    out_d = nc.dram_tensor("out", [D, TPC], F32, kind="ExternalOutput").ap()
    kerq_s = nc.dram_tensor("kerq_s", [D, TPC], BF16)
    kerk_s = nc.dram_tensor("kerk_s", [D, TPC], BF16)
    o0_s = nc.dram_tensor("o0_s", [D, TPC], BF16)
    uq_s = nc.dram_tensor("uq_s", [D, TPC], BF16)
    uk_s = nc.dram_tensor("uk_s", [3 * D, TPC], BF16)
    uv_s = nc.dram_tensor("uv_s", [3 * D, TPC], BF16)
    ug_s = nc.dram_tensor("ug_s", [D, TPC], BF16)
    ubua_s = nc.dram_tensor("ubua_s", [32, TPC], F32)
    og_s = nc.dram_tensor("og_s", [D, TPC], BF16)
    olpw_s = nc.dram_tensor("olpw_s", [8, 128, NCHD * 64], BF16)
    cc1a = nc.dram_tensor("cc1a", [1024, 528], F32)
    cc1b = nc.dram_tensor("cc1b", [1024, 528], F32)
    cc2a = nc.dram_tensor("cc2a", [24, 7168], F32)
    cc2b = nc.dram_tensor("cc2b", [24, 7168], F32)
    cc3a = nc.dram_tensor("cc3a", [1024, 2048], F32)
    cc3b = nc.dram_tensor("cc3b", [1024, 2048], F32)
    cqflat_s = nc.dram_tensor("cqflat_s", [1, NCHD * 32], F32)
    rowscr_s = nc.dram_tensor("rowscr_s", [1, TPC], F32).ap()

    x_sb = sb("x_sb", [128, 16384])
    xbf = sb("xbf", [128, 16384], BF16)
    wbuf = sb("wbuf", [128, 16384], BF16)
    fbuf = sb("fbuf", [128, 8192])
    ysm = sb("ysm", [128, 4608], BF16)
    cst = sb("cst", [128, 16])
    crow = sb("crow_sb", [1, 128])
    identb = sb("identb_sb", [128, 128], BF16)
    identf = sb("identf_sb", [128, 128])
    cmask = sb("cmask_sb", [128, 128])
    cum32 = sb("cum32_sb", [32, 32])
    oh31 = sb("oh31_sb", [32, 1])
    strict = sb("strict_sb", [96, 96])
    amask = sb("amask_sb", [96, 32])
    rmsw = sb("rmsw_sb", [128, 1])
    onesb = sb("onesb_sb", [128, 1], BF16)
    pm8 = sb("pm8", [128, 8])
    hm8 = sb("hm8", [128, 8])
    ipm8 = sb("ipm8", [128, 8])
    cp = ConstPool(nc, cst)
    bqkv = feat_rows(nc, P, "bqkv_sb", bqkv_d, 3 * D, stack)
    gqc = feat_rows(nc, P, "gq_sb", gq_d, D, stack)
    bqc = feat_rows(nc, P, "bq_sb", bq_d, D, stack)
    gkc = feat_rows(nc, P, "gk_sb", gk_d, D, stack)
    bkc = feat_rows(nc, P, "bk_sb", bk_d, D, stack)
    boc = feat_rows(nc, P, "bo_sb", bo_d, D, stack)
    PS = [psf(f"PS{i}") for i in range(6)]
    PSB = [stack.enter_context(nc.psum_tensor(f"PSB{i}", [128, 512], BF16))
           for i in range(2)]

    P.dma("scalar", cst[:], consts_d)
    P.dma("scalar", crow[:], crow_d)
    P.dma("gpsimd", identb[:], ident_d)
    P.dma("scalar", identf[:], ident_d)
    P.dma("scalar", cmask[:], cmask_d)
    P.dma("scalar", cum32[:], cum32_d)
    P.dma("scalar", oh31[:], oh31_d)
    P.dma("scalar", strict[:], strict_d)
    P.dma("scalar", amask[:], amask_d)
    P.dma("scalar", onesb[:], onesb_d)
    P.dma("scalar", rmsw[:], rmsw_d.rearrange("(a p) -> p a", p=128),
          allow_slow_non_contiguous=True)
    for c in range(8):
        P.dma("sync", x_sb[:, c * TPC:(c + 1) * TPC], xT_d[c * 128:(c + 1) * 128, :])
    pm1 = sb("pm1", [1, 8])
    hm1 = sb("hm1", [1, 8])
    P.dma("scalar", pm1[:], pmask_d)
    P.dma("scalar", hm1[:], hmask_d)
    p_mm(P, PS[0][:, 0:8], crow[:], pm1[:])
    p_copy(P, "vector", pm8[:], PS[0][:, 0:8])
    p_mm(P, PS[0][:, 8:16], crow[:], hm1[:])
    p_copy(P, "vector", hm8[:], PS[0][:, 8:16])
    p_ts(P, ipm8[:], pm8[:], cp.col(-1.0), cp.col(1.0), ALU.mult, ALU.add)

    scr = dict(sq0=fbuf[:, 0:256], sq1=fbuf[:, 256:512],
               r_m=fbuf[0:1, 512:768], r_ss=fbuf[0:1, 768:1024],
               r_a=fbuf[0:1, 1024:1280], r_b=fbuf[0:1, 1280:1536],
               r_t=fbuf[0:1, 1536:1792],
               A2=fbuf[:, 2048:2304], B2=fbuf[:, 2304:2560])

    def emit_ln(dst_bf, x32_dst=None, gate_cb=None, eps=1e-5):
        for t8 in range(8):
            xc = [x_sb[:, c * TPC + t8 * 256: c * TPC + t8 * 256 + 256]
                  for c in range(8)]
            scr["PS_sum"] = PS[4][0:1, 0:256]
            scr["PS_sq"] = PS[5][0:1, 0:256]
            scr["PS_A"] = PS[0][:, 0:256]
            scr["PS_B"] = PS[1][:, 0:256]
            emit_stats(P, cp, crow, xc, D, 0, scr, eps)
            for s in range(2):
                st8 = t8 * 2 + s
                ssl = slice(s * 128, s * 128 + 128)
                for c in range(8):
                    x32 = fbuf[:, 2560 + c * 128: 2560 + (c + 1) * 128]
                    p_tt(P, scr["sq0"][:, 0:128], xc[c][:, ssl], scr["A2"][:, ssl],
                         ALU.mult)
                    p_tt(P, x32, scr["sq0"][:, 0:128], scr["B2"][:, ssl], ALU.add)
                    p_copy(P, "scalar",
                           dst_bf[:, c * TPC + st8 * 128: c * TPC + st8 * 128 + 128],
                           x32)
                if gate_cb is not None:
                    gate_cb(st8, fbuf[:, 2560:2560 + 1024])

    emit_ln(xbf)
    qkv_s = nc.dram_tensor("qkv_s", [3 * D, TPC], BF16)
    stat4 = fbuf[0:1, 0:8192]
    evb = sb("evb", [128, 1024])
    sqb = sb("sqb", [128, 1024])
    p_memset(P, "vector", stat4, 0.0)
    wq_slot = [wbuf[:, 14336 + i * 1024: 14336 + (i + 1) * 1024] for i in range(2)]
    for mc in range(24):
        slot = wq_slot[mc % 2]
        for kc in range(8):
            P.dma("gpsimd", slot[:, kc * 128:(kc + 1) * 128],
                  Wqkv_d[kc * 128:(kc + 1) * 128, mc * 128:(mc + 1) * 128],
                  tag=f"wq{mc}")
        for t in range(4):
            bank = PS[t % 2][:, 0:512]
            for kc in range(8):
                p_mm(P, bank, slot[:, kc * 128:(kc + 1) * 128],
                     xbf[:, kc * TPC + t * 512: kc * TPC + t * 512 + 512],
                     start=(kc == 0), stop=(kc == 7), tag="qkvmm")
            ev = evb[:, (t % 2) * 512:(t % 2) * 512 + 512]
            p_sadd(P, ev, bank, bqkv[:, mc:mc + 1], tag="qkvev")
            if mc < 16:
                qk = 0 if mc < 8 else 2
                p_mm(P, PS[2][0:1, 0:512], cp.col(1.0), ev, tag="stsum")
                p_stt(P, stat4[0:1, qk * TPC + t * 512: qk * TPC + t * 512 + 512],
                      PS[2][0:1, 0:512], cp.col(1.0, 1),
                      stat4[0:1, qk * TPC + t * 512: qk * TPC + t * 512 + 512],
                      ALU.mult, ALU.add, tag="stacc")
                sqt = sqb[:, (t % 2) * 512:(t % 2) * 512 + 512]
                p_act(P, sqt, ev, AF.Square)
                p_mm(P, PS[3][0:1, 0:512], cp.col(1.0), sqt, tag="stsq")
                p_stt(P, stat4[0:1, (qk + 1) * TPC + t * 512: (qk + 1) * TPC + t * 512 + 512],
                      PS[3][0:1, 0:512], cp.col(1.0, 1),
                      stat4[0:1, (qk + 1) * TPC + t * 512: (qk + 1) * TPC + t * 512 + 512],
                      ALU.mult, ALU.add, tag="stacc2")
            stg = ysm[:, (t % 2) * 512:(t % 2) * 512 + 512]
            p_copy(P, "vector", stg, ev, tag="qkvbf")
            P.dma("sync", qkv_s[mc * 128:(mc + 1) * 128, t * 512:(t + 1) * 512],
                  stg, tag="qkvsp")
    ab_rows = stat4
    for qk in range(2):
        for hh2 in range(2):
            cs = slice(2 * qk * TPC + hh2 * 1024, 2 * qk * TPC + hh2 * 1024 + 1024)
            cq2 = slice((2 * qk + 1) * TPC + hh2 * 1024,
                        (2 * qk + 1) * TPC + hh2 * 1024 + 1024)
            srow = stat4[0:1, cs]
            qrow = stat4[0:1, cq2]
            rmr = evb[0:1, 0:1024]
            rtr = sqb[0:1, 0:1024]
            p_act(P, rmr, srow, AF.Copy, scale=1.0 / D)
            p_act(P, rtr, rmr, AF.Square)
            p_stt(P, rtr, rtr, cp.col(-float(D), 1), qrow, ALU.mult, ALU.add)
            p_act(P, rtr, rtr, AF.Sqrt, scale=1.0 / (D - 1))
            p_ts(P, rtr, rtr, cp.col(1e-6, 1), None, ALU.add)
            p_recip(P, srow, rtr)
            p_stt(P, qrow, rmr, cp.col(-1.0, 1), srow, ALU.mult, ALU.mult)
    for qk in range(2):
        gam = gqc if qk == 0 else gkc
        bet_ = bqc if qk == 0 else bkc
        dst_s = kerq_s if qk == 0 else kerk_s
        for fc in range(8):
            qb = ysm[:, 0:2048]
            P.dma("sync", qb, qkv_s[(qk * 8 + fc) * 128:(qk * 8 + fc + 1) * 128, :],
                  tag="fmld")
            for t in range(4):
                tok = slice(t * 512, (t + 1) * 512)
                p_mm(P, PS[2 + t % 2][:, 0:512], crow[:],
                     ab_rows[0:1, 2 * qk * TPC + t * 512: 2 * qk * TPC + t * 512 + 512],
                     tag="abc")
                p_mm(P, PS[4 + t % 2][:, 0:512], crow[:],
                     ab_rows[0:1, (2 * qk + 1) * TPC + t * 512:
                             (2 * qk + 1) * TPC + t * 512 + 512], tag="bbc")
                t1 = evb[:, (t % 2) * 512:(t % 2) * 512 + 512]
                t2 = sqb[:, (t % 2) * 512:(t % 2) * 512 + 512]
                p_tt(P, t1, qb[:, tok], PS[2 + t % 2][:, 0:512], ALU.mult)
                p_tt(P, t2, t1, PS[4 + t % 2][:, 0:512], ALU.add)
                p_ts(P, t1, t2, gam[:, fc:fc + 1], bet_[:, fc:fc + 1],
                     ALU.mult, ALU.add)
                stg = ysm[:, 2048 + (t % 2) * 512: 2048 + (t % 2) * 512 + 512]
                p_act(P, stg, t1, AF.Square)
                P.dma("sync", dst_s[fc * 128:(fc + 1) * 128, tok], stg, tag="fmsp")

    ohm1 = sb("ohm1", [1, 8])
    ohm_d = nc.dram_tensor("ohm", [1, 8], F32, kind="ExternalInput").ap()
    P.dma("scalar", ohm1[:], ohm_d)
    ohm8 = sb("ohm8", [128, 8])
    p_mm(P, PS[0][:, 16:24], crow[:], ohm1[:])
    p_copy(P, "vector", ohm8[:], PS[0][:, 16:24])
    hsel_d = nc.dram_tensor("hsel", [24, 3], F32, kind="ExternalInput").ap()
    hsel = sb("hsel_sb", [24, 3])
    P.dma("scalar", hsel[:], hsel_d)

    stg1 = sb("stg1", [128, 8 * 66])
    pref = sb("pref", [128, 8 * 66])
    stg1m = pref
    p_memset(P, "vector", stg1[:], 0.0)
    KTt = wbuf[:, 2048:2176]
    VTt = wbuf[:, 2176:2304]
    for p in range(8):
        kkp = ysm[:, 2048:4096]
        vp = wbuf[:, 0:2048]
        P.dma("sync", kkp, kerk_s[p * 128:(p + 1) * 128, :], tag=f"b1kk{p}")
        P.dma("sync", vp, qkv_s[(16 + p) * 128:(17 + p) * 128, :], tag=f"b1v{p}")
        for c in range(16):
            ch = slice(c * 128, (c + 1) * 128)
            p_trans(P, PSB[0][:, 0:128], kkp[:, ch], identb[:], tag="b1trk")
            p_copy(P, "vector", KTt, PSB[0][:, 0:128])
            p_trans(P, PSB[1][:, 0:128], vp[:, ch], identb[:], tag="b1trv")
            p_copy(P, "vector", VTt, PSB[1][:, 0:128])
            for h in range(2):
                hp = slice(64 * h, 64 * h + 64)
                p_mmtp(P, PS[2][hp, 0:64], KTt[:, hp], VTt[:, hp],
                       True, True, (0, 64 * h), tag="b1S")
                p_mmtp(P, PS[2][hp, 64:65], KTt[:, hp], onesb[:],
                       True, True, (0, 64 * h), tag="b1s")
                p_tt(P, stg1[hp, p * 66:p * 66 + 65], stg1[hp, p * 66:p * 66 + 65],
                     PS[2][hp, 0:65], ALU.add, tag="b1acc")
    for r in range(8):
        p_ts(P, stg1m[:], stg1[:], ohm8[:, r:r + 1], None, ALU.mult, tag="b1m")
        P.dma("sync", cc1a[r * 128:(r + 1) * 128, 0:528], stg1m[:, 0:528],
              tag=f"b1sl{r}")
    p_cc(P, "AllReduce", ALU.add, cc1a, cc1b, tag="CC1")
    p_memset(P, "vector", pref[:], 0.0)
    for r in range(8):
        P.dma("sync", stg1[:, 0:528], cc1b[r * 128:(r + 1) * 128, 0:528],
              tag=f"b2ld{r}")
        p_stt(P, pref[:, 0:528], stg1[:, 0:528], pm8[:, r:r + 1], pref[:, 0:528],
              ALU.mult, ALU.add, tag="prefacc")
    S_st = sb("S_stF", [128, 8 * 64])
    S_bfB = sb("S_bfB", [128, 8 * 64], BF16)
    s_st = sb("s_stF", [128, 8])
    s_bfB = sb("s_bfB", [128, 8], BF16)
    zr = sb("zrF", [1, 128])
    zb_sb = sb("zb_sbF", [128, 128])
    atm = sb("atmF", [128, 128], BF16)
    for p in range(8):
        p_copy(P, "vector", S_st[:, p * 64:(p + 1) * 64], pref[:, p * 66:p * 66 + 64])
        p_copy(P, "vector", s_st[:, p:p + 1], pref[:, p * 66 + 64:p * 66 + 65])
        p_copy(P, "scalar", S_bfB[:, p * 64:(p + 1) * 64], S_st[:, p * 64:(p + 1) * 64])
        p_copy(P, "scalar", s_bfB[:, p:p + 1], s_st[:, p:p + 1])
    for p in range(8):
        kqp = ysm[:, 0:2048]
        kkp = ysm[:, 2048:4096]
        vp = wbuf[:, 0:2048]
        o_pair = xbf[:, 0:2048]
        P.dma("sync", kqp, kerq_s[p * 128:(p + 1) * 128, :], tag=f"b2kq{p}")
        P.dma("sync", kkp, kerk_s[p * 128:(p + 1) * 128, :], tag=f"b2kk{p}")
        P.dma("sync", vp, qkv_s[(16 + p) * 128:(17 + p) * 128, :], tag=f"b2v{p}")
        Scol0 = slice(p * 64, p * 64 + 64)
        for c in range(16):
            ch = slice(c * 128, (c + 1) * 128)
            p_trans(P, PSB[0][:, 0:128], kkp[:, ch], identb[:], tag="b2trk")
            p_copy(P, "vector", KTt, PSB[0][:, 0:128])
            p_trans(P, PSB[1][:, 0:128], vp[:, ch], identb[:], tag="b2trv")
            p_copy(P, "vector", VTt, PSB[1][:, 0:128])
            for h in range(2):
                hp = slice(64 * h, 64 * h + 64)
                pa = PS[2 + (c * 2 + h) % 2]
                po = PS[4 + (c * 2 + h) % 2]
                p_mm(P, pa[:, 0:128], kkp[hp, ch], kqp[hp, ch], tag="at")
                p_tt(P, atm[:], pa[:, 0:128], cmask[:], ALU.mult, tag="atm")
                p_mmtp(P, po[hp, 0:128], VTt[:, hp], atm[:], True, False,
                       (0, 64 * h))
                p_mmtp(P, po[hp, 0:128], S_bfB[hp, Scol0], kqp[hp, ch], False, True,
                       (64 * h, 64 * h))
                p_mm(P, PS[1][0:1, 0:128], onesb[:], atm[:], start=True, stop=False,
                     tag="z1")
                p_mm(P, PS[1][0:1, 0:128], s_bfB[hp, p:p + 1], kqp[hp, ch],
                     start=False, stop=True, tag="z2")
                p_ts(P, zr[:], PS[1][0:1, 0:128], cp.col(1e-6, 1), None, ALU.add)
                p_recip(P, zr[:], zr[:])
                p_mmtp(P, PS[0][hp, 0:128], crow[0:1, 0:64], zr[:], True, True,
                       (0, 64 * h))
                p_copy(P, "scalar", zb_sb[hp, :], PS[0][hp, 0:128])
                p_tt(P, o_pair[hp, ch], po[hp, 0:128], zb_sb[hp, :], ALU.mult,
                     tag="ozb")
                p_mmtp(P, PS[1][hp, 0:64], KTt[:, hp], VTt[:, hp], True, True,
                       (0, 64 * h), tag="Sup1")
                p_mmtp(P, PS[1][hp, 64:65], KTt[:, hp], onesb[:], True, True,
                       (0, 64 * h), tag="Sup2")
                p_tt(P, S_st[hp, Scol0], S_st[hp, Scol0], PS[1][hp, 0:64], ALU.add)
                p_tt(P, s_st[hp, p:p + 1], s_st[hp, p:p + 1], PS[1][hp, 64:65],
                     ALU.add)
                p_copy(P, "scalar", S_bfB[hp, Scol0], S_st[hp, Scol0])
                p_copy(P, "scalar", s_bfB[hp, p:p + 1], s_st[hp, p:p + 1])
        P.dma("sync", o0_s[p * 128:(p + 1) * 128, :], o_pair, tag=f"b2o{p}")

    if phases < 3:
        nc2 = P.finalize()
        stack.close()
        return nc2, cp.vals
    gwT = sb("gwT_sbF", [128, 64])
    gbias = sb("gbiasF", [128, 8])
    logits = sb("logitsF", [128, 8])
    m8 = sb("m8F", [128, 1])
    mask_all = sb("mask_allF", [128, 16 * 8])
    b2bc = evb
    b2row = sb("b2rowF", [1, 1024])
    b1e = sb("b1eF", [128, 32])
    h2 = sb("h2F", [128, 4 * 256], BF16)
    wb_sb = sb("wb_sbF", [128, 8 * 24])
    wa_sb2 = sb("wa_sb2F", [128, 8 * 8])
    ohsel3_d = nc.dram_tensor("ohsel3", [3, 24], F32, kind="ExternalInput").ap()
    ohsel3 = sb("ohsel3_sb", [3, 24])
    P.dma("scalar", ohsel3[:], ohsel3_d)
    gbrow = sb("gbrowF", [1, 8])
    ebrow = sb("ebrowF", [1, 8])
    P.dma("scalar", gbrow[:], gb_d)
    P.dma("scalar", ebrow[:], eb_d)
    P.dma("scalar", gwT[:].rearrange("p (kc e) -> p kc e", e=8),
          gwT_d.rearrange("(kc p) e -> p kc e", p=128),
          allow_slow_non_contiguous=True)
    P.dma("scalar", wb_sb[:].rearrange("p (kc e) -> p kc e", e=24),
          Wb_d.rearrange("(kc p) e -> p kc e", p=128),
          allow_slow_non_contiguous=True)
    P.dma("scalar", wa_sb2[:].rearrange("p (kc e) -> p kc e", e=8),
          Wa_d.rearrange("(kc p) e -> p kc e", p=128),
          allow_slow_non_contiguous=True)
    p_mm(P, PS[0][:, 24:32], crow[:], gbrow[:], start=True, stop=False)
    p_mm(P, PS[0][:, 24:32], crow[:], ebrow[:], start=False, stop=True)
    p_copy(P, "vector", gbias[:], PS[0][:, 24:32])
    for kc in range(8):
        P.dma("gpsimd", wbuf[:, kc * 1024:(kc + 1) * 1024],
              Wo0_d[kc * 128:(kc + 1) * 128, :], tag=f"ldwo0{kc}")
    for c in range(8):
        P.dma("sync", xbf[:, c * TPC:(c + 1) * TPC], o0_s[c * 128:(c + 1) * 128, :],
              tag="ldo0")
    for t in range(4):
        for mc in range(8):
            bank = PS[mc % 4][:, 0:512]
            for kc in range(8):
                p_mm(P, bank, wbuf[:, kc * 1024 + mc * 128: kc * 1024 + mc * 128 + 128],
                     xbf[:, kc * TPC + t * 512: kc * TPC + t * 512 + 512],
                     start=(kc == 0), stop=(kc == 7))
            xs = x_sb[:, mc * TPC + t * 512: mc * TPC + t * 512 + 512]
            p_stt(P, xs, bank, boc[:, mc:mc + 1], xs, ALU.add, ALU.add, tag="res0")

    def gate_cb(st8, x32_all):
        for kc in range(8):
            p_mm(P, PS[2][:, 16:24], x32_all[:, kc * 128:(kc + 1) * 128],
                 gwT[:, kc * 8:(kc + 1) * 8], start=(kc == 0), stop=(kc == 7),
                 tag="gate")
        p_tt(P, logits[:], PS[2][:, 16:24], gbias[:], ALU.add)
        p_red(P, m8[:], logits[:], ALU.max)
        p_ts(P, mask_all[:, st8 * 8:(st8 + 1) * 8], logits[:], m8[:, 0:1],
             None, ALU.is_ge)
    emit_ln(xbf, gate_cb=gate_cb)

    def load_eighth(e, q, slot):
        for kc in range(8):
            P.dma("gpsimd", wbuf[:, slot * 8192 + kc * 512:(kc + 1) * 512 + slot * 8192],
                  w1_d[e, kc * 128:(kc + 1) * 128, q * 512:(q + 1) * 512],
                  tag=f"w1e{e}q{q}")
        for kc in range(4):
            P.dma("gpsimd",
                  wbuf[:, slot * 8192 + 4096 + kc * 1024: slot * 8192 + 4096 + (kc + 1) * 1024],
                  w2_d[e, q * 512 + kc * 128: q * 512 + (kc + 1) * 128, :],
                  tag=f"w2e{e}q{q}")

    load_eighth(0, 0, 1)
    seq = 0
    for g in range(4):
        p_memset(P, "vector", ysm[:, 0:4096], 0.0)
        for e in range(8):
            P.dma("scalar", b1e[:], b1_d[e].rearrange("(c p) -> p c", p=128),
                  allow_slow_non_contiguous=True, tag=f"b1 e{e}")
            P.dma("scalar", b2row[:], b2_d[e:e + 1, :], tag=f"b2 e{e}")
            for hf in range(2):
                p_mm(P, PS[4][:, 0:512], crow[:], b2row[0:1, hf * 512:(hf + 1) * 512],
                     tag="b2bc")
                p_copy(P, "vector", b2bc[:, hf * 512:(hf + 1) * 512], PS[4][:, 0:512])
            for q in range(8):
                slot = (seq + 1) % 2
                seq += 1
                if not (g == 3 and e == 7 and q == 7):
                    ng, ne, nq = g, e, q + 1
                    if nq == 8:
                        nq = 0
                        ne += 1
                        if ne == 8:
                            ne, ng = 0, g + 1
                    load_eighth(ne, nq, (seq + 1) % 2)
                for tt in range(2):
                    tok0 = g * 512 + tt * 256
                    for dc in range(4):
                        bank = PS[4 + dc % 2][:, 0:256]
                        for kc in range(8):
                            p_mm(P, bank,
                                 wbuf[:, slot * 8192 + kc * 512 + dc * 128: slot * 8192 + kc * 512 + dc * 128 + 128],
                                 xbf[:, kc * TPC + tok0: kc * TPC + tok0 + 256],
                                 start=(kc == 0), stop=(kc == 7), tag="moe1")
                        p_act(P, h2[:, dc * 256:(dc + 1) * 256], bank, AF.Relu,
                              bias=b1e[:, (q * 4 + dc):(q * 4 + dc) + 1], tag="hrelu")
                    for s2 in range(2):
                        st8 = (tok0 // 128) + s2
                        mcol = mask_all[:, st8 * 8 + e: st8 * 8 + e + 1]
                        for hf in range(2):
                            bank = PS[s2 * 2 + hf][:, 0:512]
                            for kc in range(4):
                                p_mm(P, bank,
                                     h2[:, kc * 256 + s2 * 128: kc * 256 + s2 * 128 + 128],
                                     wbuf[:, slot * 8192 + 4096 + kc * 1024 + hf * 512: slot * 8192 + 4096 + kc * 1024 + hf * 512 + 512],
                                     start=(kc == 0), stop=(kc == 3), tag="moe2")
                            ya = ysm[:, (tt * 2 + s2) * 1024 + hf * 512:
                                     (tt * 2 + s2) * 1024 + hf * 512 + 512]
                            p_stt(P, ya, bank, mcol, ya, ALU.mult, ALU.add,
                                  tag="yflush")
            for st in range(4):
                st8 = g * 4 + st
                mcol = mask_all[:, st8 * 8 + e: st8 * 8 + e + 1]
                for hf in range(2):
                    ya = ysm[:, st * 1024 + hf * 512: st * 1024 + hf * 512 + 512]
                    p_stt(P, ya, b2bc[:, hf * 512:(hf + 1) * 512], mcol, ya,
                          ALU.mult, ALU.add, tag="yb2")
        for st in range(4):
            st8 = g * 4 + st
            for c in range(8):
                p_trans(P, PSB[0][:, 0:128],
                        ysm[:, st * 1024 + c * 128: st * 1024 + c * 128 + 128],
                        identb[:], tag="ytr")
                xs = x_sb[:, c * TPC + st8 * 128: c * TPC + st8 * 128 + 128]
                p_tt(P, xs, xs, PSB[0][:, 0:128], ALU.add, tag="res1")

    def ba_cb(st8, x32_all):
        for kc in range(8):
            p_mm(P, PS[2][0:24, 128:256], wb_sb[:, kc * 24:(kc + 1) * 24],
                 x32_all[:, kc * 128:(kc + 1) * 128], start=(kc == 0),
                 stop=(kc == 7), tag="ubm")
            p_mm(P, PS[3][0:8, 128:256], wa_sb2[:, kc * 8:(kc + 1) * 8],
                 x32_all[:, kc * 128:(kc + 1) * 128], start=(kc == 0),
                 stop=(kc == 7), tag="uam")
        p_copy(P, "vector", fbuf[0:24, 4096 + st8 * 128: 4096 + (st8 + 1) * 128],
               PS[2][0:24, 128:256])
        p_copy(P, "vector", fbuf[0:8, 6144 + st8 * 128: 6144 + (st8 + 1) * 128],
               PS[3][0:8, 128:256])
    emit_ln(xbf, gate_cb=ba_cb)
    P.dma("sync", ubua_s[0:24, :], fbuf[0:24, 4096:6144], tag="spub")
    P.dma("sync", ubua_s[24:32, :], fbuf[0:8, 6144:8192], tag="spua")

    projs = [(uq_s, Wq_d, 8, True), (uk_s, Wk_d, 24, True), (uv_s, Wv_d, 24, True),
             (ug_s, Wg_d, 8, False)]
    pslot = [wbuf[:, 14336 + i * 1024: 14336 + (i + 1) * 1024] for i in range(2)]
    pi = 0
    pc_halo = 0
    for out_s, W_d, nmc, has_halo in projs:
        for mc in range(nmc):
            slot = pslot[pi % 2]
            for kc in range(8):
                P.dma("gpsimd", slot[:, kc * 128:(kc + 1) * 128],
                      W_d[kc * 128:(kc + 1) * 128, mc * 128:(mc + 1) * 128],
                      tag=f"ldp{pi}")
            for t in range(4):
                bank = PS[t % 4][:, 0:512]
                for kc in range(8):
                    p_mm(P, bank, slot[:, kc * 128:(kc + 1) * 128],
                         xbf[:, kc * TPC + t * 512: kc * TPC + t * 512 + 512],
                         start=(kc == 0), stop=(kc == 7), tag="proj")
                ob = ysm[:, (t % 2) * 512:(t % 2) * 512 + 512]
                p_copy(P, "scalar", ob, bank)
                P.dma("sync", out_s[mc * 128:(mc + 1) * 128, t * 512:(t + 1) * 512],
                      ob, tag=f"stp{pi}")
                if has_halo and t == 3:
                    p_trans(P, PSB[1][0:3, 128:256], ob[:, 509:512], identb[:],
                            tag="htr")
                    p_copy(P, "vector",
                           fbuf[0:3, pc_halo * 128:(pc_halo + 1) * 128],
                           PSB[1][0:3, 128:256])
                    pc_halo += 1
            pi += 1
    for chk in range(14):
        p_mm(P, PS[5][0:24, 0:512], ohsel3[:], fbuf[0:3, chk * 512:(chk + 1) * 512],
             tag="hsl")
        p_copy(P, "vector", fbuf[0:24, 7168:7680], PS[5][0:24, 0:512])
        P.dma("sync", cc2a[0:24, chk * 512:(chk + 1) * 512], fbuf[0:24, 7168:7680],
              tag="hst")
    p_cc(P, "AllReduce", ALU.add, cc2a, cc2b, tag="CC2")
    haloT = sb("haloT", [128, 56 * 3], BF16)
    for chk in range(14):
        P.dma("sync", fbuf[0:24, 7168:7680], cc2b[0:24, chk * 512:(chk + 1) * 512],
              tag="hld")
        p_mm(P, PS[5][0:3, 0:512], hsel[:], fbuf[0:24, 7168:7680], tag="hpick")
        p_copy(P, "vector", fbuf[0:3, chk * 512:(chk + 1) * 512], PS[5][0:3, 0:512])
    for pc in range(56):
        p_trans(P, PS[5][0:128, 128:131], fbuf[0:3, pc * 128:(pc + 1) * 128],
                identf[0:3, 0:3], tag="htr2")
        p_copy(P, "vector", haloT[:, pc * 3:(pc + 1) * 3], PS[5][0:128, 128:131])

    if phases < 4:
        nc2 = P.finalize()
        stack.close()
        return nc2, cp.vals
    NCHD2 = 64
    al = sb("alF", [1, 1])
    dtbs = sb("dtbF", [1, 1])
    bet = sb("betF", [96, NCHD2])
    nbet = sb("nbetF", [96, NCHD2])
    nbb = sb("nbbF", [96, NCHD2])
    bb = sb("bbF", [96, NCHD2])
    csub = sb("csubF", [96, NCHD2])
    ecs = sb("ecsF", [96, NCHD2])
    egc = sb("egcF", [96, NCHD2])
    egb = sb("egbF", [128, NCHD2])
    eg_row = fbuf[0:1, 7168:7232]
    gam_row = fbuf[0:1, 7232:7296]
    g32 = sb("g32F", [32, NCHD2])
    csubT = sb("csubTF", [64, 96])
    ecqb = ysm[:, 4096:4608]
    growbuf = fbuf[0:1, 4096:6144]
    rr = fbuf[0:1, 6144:6656]
    KT = sb("KTF", [96, 128], BF16)
    VT = sb("VTF", [96, 128], BF16)
    KB = sb("KBF", [96, 128], BF16)
    KC2 = sb("KC2F", [96, 128], BF16)
    v1ext = stg1[0:96, 0:256]
    D1 = pref[0:96, 256:352]
    GM = pref[0:96, 352:448]
    Nt = [sb(f"NtF{i}", [96, 96], BF16) for i in range(2)]
    Bt = [sb(f"BtF{i}", [96, 96], BF16) for i in range(2)]
    Xb = [sb(f"XbF{i}", [96, 384], BF16) for i in range(2)]
    Am = sb("AmF", [96, 32], BF16)
    Qb = sb("QbF", [128, 32], BF16)
    em = sb("emF", [96, 32])
    S_comb = stg1[:, 264:520]
    S_cbf = sb("S_cbfF", [128, 256], BF16)
    t96 = sqb[0:96, 0:96]
    olst = ysm[:, 0:512]
    mstg = pref[:, 0:256]
    S_in8 = sqb
    S_in8b = h2
    r_row = fbuf[0:1, 6656:7168]

    qc = xbf[:, 0:2048]
    Kc = xbf[:, 2048:8192]
    Vc = xbf[:, 8192:14336]
    ugp = xbf[:, 14336:16384]
    Kc_v = Kc.rearrange("p (c s) -> p c s", s=96)
    Vc_v = Vc.rearrange("p (c s) -> p c s", s=96)
    p_memset(P, "vector", v1ext, 0.0)

    for hd in range(8):
        cw = fbuf[:, 7680:7708]
        P.dma("scalar", cw[:, 0:4], cq_w[hd])
        for i in range(3):
            P.dma("scalar", cw[:, 4 + i * 4: 8 + i * 4], ck_w[hd, i])
            P.dma("scalar", cw[:, 16 + i * 4: 20 + i * 4], cv_w[hd, i])
        P.dma("scalar", al[:], alog_d[0:1, hd:hd + 1])
        P.dma("scalar", dtbs[:], dtb_d[0:1, hd:hd + 1])
        p_act(P, al[:], al[:], AF.Exp)
        p_ts(P, al[:], al[:], cp.col(-1.0, 1), None, ALU.mult)
        grow = growbuf
        P.dma("sync", grow, ubua_s[24 + hd:25 + hd, :], tag="ldua")
        p_ts(P, grow, grow, dtbs[0:1, 0:1], None, ALU.add)
        p_act(P, fbuf[0:1, 0:2048], grow, AF.Abs)
        p_act(P, fbuf[0:1, 0:2048], fbuf[0:1, 0:2048], AF.Exp, scale=-1.0)
        p_act(P, fbuf[0:1, 0:2048], fbuf[0:1, 0:2048], AF.Ln, bias=cp.col(1.0, 1))
        p_act(P, fbuf[0:1, 2048:4096], grow, AF.Relu)
        p_tt(P, grow, fbuf[0:1, 2048:4096], fbuf[0:1, 0:2048], ALU.add)
        p_ts(P, grow, grow, al[0:1, 0:1], None, ALU.mult)
        P.dma("sync", rowscr_s, grow, tag="gsp")
        P.dma("sync", fbuf[0:64, 4096:4128],
              rowscr_s.rearrange("o (c t) -> (o c) t", t=32), tag="gfold")
        p_trans(P, PS[0][0:32, 0:64], fbuf[0:64, 4096:4128], identf[0:64, 0:64],
                tag="gtr")
        p_copy(P, "vector", g32[:], PS[0][0:32, 0:64])
        p_mm(P, PS[1][0:32, 0:NCHD2], cum32[:], g32[:], tag="cum")
        for r3 in range(3):
            p_copy(P, "vector", csub[32 * r3:32 * r3 + 32, :], PS[1][0:32, 0:NCHD2])
        p_mm(P, PS[2][0:1, 0:NCHD2], oh31[:], csub[0:32, :], tag="gam")
        p_copy(P, "vector", gam_row, PS[2][0:1, 0:NCHD2])
        p_act(P, eg_row, gam_row, AF.Exp)
        p_mm(P, PS[3][:, 0:NCHD2], crow[:], eg_row, tag="egb")
        p_copy(P, "vector", egb[:], PS[3][:, 0:NCHD2])
        p_mm(P, PS[4][0:96, 0:NCHD2], crow[0:1, 0:96], gam_row, tag="gb96")
        p_tt(P, egc[:], PS[4][0:96, 0:NCHD2], csub[:], ALU.subtract)
        p_act(P, egc[:], egc[:], AF.Exp)
        p_act(P, ecs[:], csub[:], AF.Exp)
        for i in range(3):
            br = fbuf[0:1, 0:2048]
            P.dma("sync", br, ubua_s[i * 8 + hd: i * 8 + hd + 1, :], tag="ldub")
            p_act(P, br, br, AF.Sigmoid)
            p_ts(P, br, br, cp.col(2.0, 1), None, ALU.mult)
            P.dma("sync", rowscr_s, br, tag="bsp")
            P.dma("sync", fbuf[0:64, 4096:4128],
                  rowscr_s.rearrange("o (c t) -> (o c) t", t=32), tag="bfold")
            p_trans(P, PS[0][0:32, 0:64], fbuf[0:64, 4096:4128], identf[0:64, 0:64],
                    tag="btr")
            p_copy(P, "vector", bet[32 * i:32 * i + 32, :], PS[0][0:32, 0:64])
        p_ts(P, nbet[:], bet[:], cp.col(-1.0, 96), None, ALU.mult)
        p_tt(P, bb[:], bet[:], ecs[:], ALU.mult)
        p_ts(P, nbb[:], bb[:], cp.col(-1.0, 96), None, ALU.mult)
        p_trans(P, PS[0][0:64, 0:96], csub[:], identf[0:96, 0:96], tag="cstr")
        p_copy(P, "vector", csubT[:], PS[0][0:64, 0:96])
        P.dma("sync", cqflat_s[0:1, :].rearrange("o (r t) -> (o r) t", t=32),
              csubT[:, 0:32], tag="cqsp")
        for seg in range(2):
            us = wbuf[:, (seg % 2) * 7189: (seg % 2) * 7189 + 7189]
            planes = [(uq_s, hd, 0)] + [(uk_s, i * 8 + hd, 1 + i) for i in range(3)] \
                + [(uv_s, i * 8 + hd, 4 + i) for i in range(3)]
            for (src_s, row, pl) in planes:
                base = pl * 1027
                if seg == 0:
                    pc = row if pl == 0 else (8 + (pl - 1) * 8 * 0 + row + 8 * 0)
                    pc = row if pl == 0 else (8 + row if pl <= 3 else 32 + row)
                    p_copy(P, "vector", us[:, base:base + 3],
                           haloT[:, pc * 3:(pc + 1) * 3], tag="haloc")
                    P.dma("sync", us[:, base + 3:base + 1027],
                          src_s[row * 128:(row + 1) * 128, 0:1024], tag="lduseg")
                else:
                    P.dma("sync", us[:, base:base + 1027],
                          src_s[row * 128:(row + 1) * 128, 1021:2048], tag="lduseg")
            for pl in range(7):
                base = pl * 1027
                cvt = fbuf[:, 0:1024]
                p_ts(P, cvt, us[:, base:base + 1024], cw[:, pl * 4:pl * 4 + 1],
                     None, ALU.mult, tag="cv0")
                for i in range(1, 4):
                    p_stt(P, cvt, us[:, base + i:base + i + 1024],
                          cw[:, pl * 4 + i:pl * 4 + i + 1], cvt, ALU.mult, ALU.add,
                          tag=f"cv{i}")
                if pl == 0:
                    dst = qc[:, seg * 1024:(seg + 1) * 1024]
                elif pl <= 3:
                    dst = Kc_v[:, 32 * seg:32 * (seg + 1), 32 * (pl - 1):32 * pl]
                else:
                    dst = Vc_v[:, 32 * seg:32 * (seg + 1), 32 * (pl - 4):32 * (pl - 3)]
                p_act(P, dst, cvt, AF.Silu, tag="silu")
        def l2n(slices, extra):
            for dsl in slices:
                p_act(P, fbuf[:, 0:512], dsl, AF.Square)
                p_mm(P, PS[5][0:1, 0:512], cp.col(1.0), fbuf[:, 0:512], tag="l2mm")
                p_act(P, r_row, PS[5][0:1, 0:512], AF.Ln, bias=cp.col(1e-6, 1))
                p_act(P, r_row, r_row, AF.Exp, scale=-0.5)
                if extra != 1.0:
                    p_ts(P, r_row, r_row, cp.col(extra, 1), None, ALU.mult)
                p_mm(P, PS[4][:, 0:512], crow[:], r_row, tag="l2bc")
                p_tt(P, dsl, dsl, PS[4][:, 0:512], ALU.mult, tag="l2ap")
        l2n([qc[:, i * 512:(i + 1) * 512] for i in range(4)], float(128.0 ** -0.5))
        l2n([Kc[:, i * 512:(i + 1) * 512] for i in range(12)], 1.0)
        p_memset(P, "vector", stg1[:, 264:392], 0.0)
        p_copy(P, "vector", stg1[:, 392:520], identf[:])
        p_copy(P, "scalar", S_cbf[:], S_comb)
        for c in range(NCHD2):
            if c % 16 == 0:
                P.dma("sync", rr,
                      cqflat_s[0:1, (c // 16) * 512:(c // 16) * 512 + 512], tag="ldcq")
                p_act(P, rr, rr, AF.Exp)
                p_mm(P, PS[3][:, 0:512], crow[:], rr, tag="ecqb")
                p_copy(P, "scalar", ecqb, PS[3][:, 0:512])
            Kch = Kc[:, c * 96:(c + 1) * 96]
            Vch = Vc[:, c * 96:(c + 1) * 96]
            Qch = qc[:, c * CT:(c + 1) * CT]
            ccol = csub[:, c:c + 1]
            p_mm(P, PS[0][0:1, 0:96], identf[0:64, c:c + 1], csubT[:], tag="sel")
            p_copy(P, "vector", rr[0:1, 0:96], PS[0][0:1, 0:96])
            p_mm(P, PS[0][0:96, 0:96], crow[0:1, 0:96], rr[0:1, 0:96], tag="bc")
            p_ts(P, t96, PS[0][0:96, 0:96], ccol, None, ALU.subtract, tag="d1a")
            p_act(P, t96, t96, AF.Relu)
            p_act(P, D1, t96, AF.Exp, scale=-1.0)
            p_mm(P, PS[1][0:96, 0:96], Kch, Kch, tag="G")
            p_tt(P, GM, PS[1][0:96, 0:96], strict[:], ALU.mult, tag="gm")
            p_stt(P, Nt[0][:], GM, nbet[:, c:c + 1], D1, ALU.mult, ALU.mult,
                  tag="N")
            p_trans(P, PSB[0][0:96, 0:96], Nt[0][:], identb[0:96, 0:96], tag="trN")
            p_copy(P, "vector", Bt[0][:], PSB[0][0:96, 0:96])
            p_trans(P, PSB[1][0:96, 0:128], Kch, identb[:], tag="trK")
            p_copy(P, "vector", KT[:], PSB[1][0:96, 0:128])
            p_trans(P, PSB[0][0:96, 128:256], Vch, identb[:], tag="trV")
            p_copy(P, "vector", VT[:], PSB[0][0:96, 128:256])
            p_mm(P, PS[4][0:96, 0:256], Kch, S_cbf[:], tag="KS0")
            p_ts(P, stg1[0:96, 0:128], VT[:], bet[:, c:c + 1], None, ALU.mult, tag="bv")
            p_ts(P, KB[:], KT[:], bb[:, c:c + 1], None, ALU.mult, tag="kb")
            p_ts(P, KC2[:], KT[:], egc[:, c:c + 1], None, ALU.mult, tag="kc2")
            p_copy(P, "vector", Xb[0][:, 0:128], KB[:])
            p_stt(P, Xb[0][:, 128:384], PS[4][0:96, 0:256], nbb[:, c:c + 1],
                  v1ext, ALU.mult, ALU.add, tag="B2")
            for k in range(7):
                xcur, xnxt = Xb[k % 2], Xb[(k + 1) % 2]
                p_mm(P, PS[2][0:96, 0:384], identb[0:96, 0:96], xcur[:],
                     start=True, stop=False, tag="apI")
                p_mm(P, PS[2][0:96, 0:384], Bt[k % 2][:], xcur[:],
                     start=False, stop=True, tag="apN")
                p_copy(P, "vector" if k % 2 == 0 else "scalar", xnxt[:],
                       PS[2][0:96, 0:384], tag="apE")
                if k < 6:
                    a_, b_ = Nt[k % 2], Bt[k % 2]
                    p_mm(P, PS[3][0:96, 0:96], b_[:], a_[:], tag="sqA")
                    p_copy(P, "scalar", Nt[(k + 1) % 2][:], PS[3][0:96, 0:96])
                    p_mm(P, PS[5][0:96, 0:96], a_[:], b_[:], tag="sqB")
                    p_copy(P, "vector", Bt[(k + 1) % 2][:], PS[5][0:96, 0:96])
            Z = Xb[1]
            p_mm(P, PS[1][0:96, 0:32], Kch, Qch, tag="AtQ")
            p_mm(P, PS[0][0:96, 0:32], crow[0:1, 0:96], rr[0:1, 0:32], tag="bc2")
            p_ts(P, em[:], PS[0][0:96, 0:32], ccol, None, ALU.subtract, tag="ama")
            p_act(P, em[:], em[:], AF.Relu, scale=-1.0)
            p_act(P, em[:], em[:], AF.Exp, scale=-1.0)
            p_tt(P, em[:], em[:], amask[:], ALU.mult, tag="amm")
            p_tt(P, Am[:], PS[1][0:96, 0:32], em[:], ALU.mult, tag="amf")
            p_tt(P, Qb[:], Qch, ecqb[:, (c % 16) * 32:(c % 16) * 32 + 32], ALU.mult,
                 tag="qb")
            p_mm(P, PS[4][0:128, 256:288], Z[:, 128:256], Am[:], start=True,
                 stop=False, tag="olp1")
            p_mm(P, PS[4][0:128, 256:288], S_cbf[:, 0:128], Qb[:], start=False,
                 stop=True, tag="olp2")
            p_copy(P, "scalar", olst[:, (c % 8) * 64:(c % 8) * 64 + 32],
                   PS[4][0:128, 256:288])
            p_mm(P, PS[5][0:128, 96:128], S_cbf[:, 128:256], Qb[:], start=True,
                 stop=False, tag="w2c1")
            p_mm(P, PS[5][0:128, 96:128], Z[:, 256:384], Am[:], start=False,
                 stop=True, tag="w2c2")
            p_copy(P, "vector", olst[:, (c % 8) * 64 + 32:(c % 8) * 64 + 64],
                   PS[5][0:128, 96:128])
            if c % 8 == 7:
                P.dma("sync", olpw_s[hd, :, (c - 7) * 64:(c + 1) * 64], olst,
                      tag="spolp")
            p_mm(P, PS[4][0:128, 0:256], KC2[:], Z[:, 128:384], tag="Supd")
            p_stt(P, S_comb, S_comb, egb[:, c:c + 1], PS[4][0:128, 0:256],
                  ALU.mult, ALU.add, tag="Sacc")
            p_copy(P, "scalar", S_cbf[:], S_comb)
        p_copy(P, "vector", pref[:, 0:128], stg1[:, 264:392])
        p_trans(P, PS[0][0:128, 128:256], stg1[:, 392:520], identf[:], tag="Mtr")
        p_copy(P, "vector", pref[:, 128:256], PS[0][0:128, 128:256])
        for r in range(8):
            p_ts(P, fbuf[:, 1024:1280], mstg, ohm8[:, r:r + 1], None,
                 ALU.mult, tag="m3m")
            P.dma("sync", cc3a[r * 128:(r + 1) * 128, hd * 256:(hd + 1) * 256],
                  fbuf[:, 1024:1280], tag=f"m3s{r}")
    p_cc(P, "AllReduce", ALU.add, cc3a, cc3b, tag="CC3")
    for hd in range(8):
        sl = S_in8[:, hd * 128:(hd + 1) * 128]
        slb = S_in8b[:, hd * 128:(hd + 1) * 128]
        p_memset(P, "vector", sl, 0.0)
        p_copy(P, "scalar", slb, sl)
        for r in range(8):
            P.dma("sync", mstg, cc3b[r * 128:(r + 1) * 128, hd * 256:(hd + 1) * 256],
                  tag="c3ld")
            p_copy(P, "scalar", S_cbf[:, 0:128], pref[:, 128:256])
            p_mm(P, PS[0][:, 0:128], S_cbf[:, 0:128], slb, tag="cmpM")
            p_tt(P, fbuf[:, 0:128], PS[0][:, 0:128], pref[:, 0:128], ALU.add,
                 tag="cmpN")
            p_ts(P, fbuf[:, 128:256], fbuf[:, 0:128], pm8[:, r:r + 1], None,
                 ALU.mult, tag="cmpm1")
            p_stt(P, sl, sl, ipm8[:, r:r + 1], fbuf[:, 128:256], ALU.mult, ALU.add,
                  tag="cmpm2")
            p_copy(P, "scalar", slb, sl)
    for hd in range(8):
        P.dma("sync", ugp, ug_s[hd * 128:(hd + 1) * 128, :], tag="ldug")
        o_acc = ysm[:, 2048:4096]
        for cg in range(8):
            P.dma("sync", olst, olpw_s[hd, :, cg * 512:(cg + 1) * 512], tag="ldolp")
            for cc_ in range(8):
                c = cg * 8 + cc_
                p_mm(P, PS[0][0:128, 0:32], identb[:], olst[:, cc_ * 64:cc_ * 64 + 32],
                     start=True, stop=False, tag="o1")
                p_mm(P, PS[0][0:128, 0:32], S_in8b[:, hd * 128:(hd + 1) * 128],
                     olst[:, cc_ * 64 + 32:cc_ * 64 + 64], start=False, stop=True,
                     tag="o2")
                p_copy(P, "scalar", o_acc[:, c * 32:(c + 1) * 32], PS[0][0:128, 0:32])
            if cg % 2 == 1:
                gq = cg // 2
                osl = o_acc[:, gq * 512:(gq + 1) * 512]
                p_act(P, fbuf[:, 0:512], osl, AF.Square)
                p_mm(P, PS[5][0:1, 0:512], cp.col(1.0), fbuf[:, 0:512], tag="rmsmm")
                p_act(P, r_row, PS[5][0:1, 0:512], AF.Ln, scale=1.0 / 128,
                      bias=cp.col(1e-5, 1))
                p_act(P, r_row, r_row, AF.Exp, scale=-0.5)
                p_mm(P, PS[3][:, 0:512], crow[:], r_row, tag="rmsbc")
                gsil = ysm[:, 1024:1536]
                p_act(P, gsil, ugp[:, gq * 512:(gq + 1) * 512], AF.Silu)
                p_tt(P, fbuf[:, 0:512], osl, PS[3][:, 0:512], ALU.mult, tag="orms")
                p_ts(P, fbuf[:, 0:512], fbuf[:, 0:512], rmsw[:, 0:1], None, ALU.mult)
                p_tt(P, osl, fbuf[:, 0:512], gsil, ALU.mult, tag="ogate")
        P.dma("sync", og_s[hd * 128:(hd + 1) * 128, :], o_acc, tag="spog")

    if phases < 5:
        nc2 = P.finalize()
        stack.close()
        return nc2, cp.vals
    for kc in range(8):
        P.dma("gpsimd", wbuf[:, kc * 1024:(kc + 1) * 1024],
              Wo1_d[kc * 128:(kc + 1) * 128, :], tag=f"ldwo1{kc}")
    for c in range(8):
        P.dma("sync", xbf[:, c * TPC:(c + 1) * TPC], og_s[c * 128:(c + 1) * 128, :],
              tag="ldog")
    for t in range(4):
        for mc in range(8):
            bank = PS[mc % 4][:, 0:512]
            for kc in range(8):
                p_mm(P, bank, wbuf[:, kc * 1024 + mc * 128: kc * 1024 + mc * 128 + 128],
                     xbf[:, kc * TPC + t * 512: kc * TPC + t * 512 + 512],
                     start=(kc == 0), stop=(kc == 7))
            xs = x_sb[:, mc * TPC + t * 512: mc * TPC + t * 512 + 512]
            p_tt(P, xs, xs, bank, ALU.add, tag="resE0")
    emit_ln(xbf)
    acc_abs = sb("acc_absF", [128, 2])
    thr = sb("thrF", [1, 2])
    thrb = sb("thrbF", [128, 4])
    b1cols = feat_rows(nc, P, "b1colsF", b1f_d, 4096, stack)
    b2cols = feat_rows(nc, P, "b2colsF", b2f_d, D, stack)
    p_memset(P, "vector", acc_abs[:], 0.0)
    for blk_i in range(32):
        slot = fbuf[:, (blk_i % 2) * 1024:(blk_i % 2) * 1024 + 1024]
        P.dma("sync", slot, W1f_d[blk_i * 128:(blk_i + 1) * 128, :], tag=f"aw1_{blk_i}")
        p_red(P, fbuf[:, 2048:2049], slot, ALU.add, absv=True, tag="absr")
        p_tt(P, acc_abs[:, 0:1], acc_abs[:, 0:1], fbuf[:, 2048:2049], ALU.add)
    for blk_i in range(32):
        slot = fbuf[:, (blk_i % 2) * 1024:(blk_i % 2) * 1024 + 1024]
        P.dma("sync", slot, W2f_d[(blk_i // 4) * 128:(blk_i // 4 + 1) * 128,
                                  (blk_i % 4) * 1024:(blk_i % 4 + 1) * 1024],
              tag=f"aw2_{blk_i}")
        p_red(P, fbuf[:, 2048:2049], slot, ALU.add, absv=True, tag="absr2")
        p_tt(P, acc_abs[:, 1:2], acc_abs[:, 1:2], fbuf[:, 2048:2049], ALU.add)
    p_mm(P, PS[0][0:1, 0:2], cp.col(1.0), acc_abs[:], tag="abssum")
    p_act(P, thr[:], PS[0][0:1, 0:2], AF.Relu, scale=1.0 / (4096 * 1024),
          bias=cp.col(-1e-8, 1))
    p_ts(P, thr[:], thr[:], cp.col(1e-8, 1), None, ALU.add)
    p_ts(P, thr[:], thr[:], cp.col(0.5, 1), None, ALU.mult)
    p_mm(P, PS[0][:, 4:6], crow[:], thr[:], tag="thrbc")
    p_copy(P, "vector", thrb[:, 0:1], PS[0][:, 4:5])
    p_copy(P, "vector", thrb[:, 2:3], PS[0][:, 5:6])
    p_ts(P, thrb[:, 1:2], thrb[:, 0:1], cp.col(-1.0), None, ALU.mult)
    p_ts(P, thrb[:, 3:4], thrb[:, 2:3], cp.col(-1.0), None, ALU.mult)
    ternb = sb("ternbF", [128, 1024], BF16)

    def tern_tiles(raw, tcol, ncols, out_tiles):
        p_ts(P, fbuf[:, 4096:4096 + ncols], raw, thrb[:, tcol:tcol + 1], None,
             ALU.is_gt, tag="tga")
        p_ts(P, fbuf[:, 5120:5120 + ncols], raw, thrb[:, tcol + 1:tcol + 2], None,
             ALU.is_lt, tag="tgb")
        p_tt(P, ternb[:, 0:ncols], fbuf[:, 4096:4096 + ncols],
             fbuf[:, 5120:5120 + ncols], ALU.subtract, tag="tsub")
        for j in range(ncols // 128):
            p_trans(P, PSB[0][:, 0:128], ternb[:, j * 128:(j + 1) * 128], identb[:],
                    tag="ttr")
            p_copy(P, "vector", out_tiles[j], PSB[0][:, 0:128])

    for g in range(4):
        p_memset(P, "vector", ysm[:, 0:4096], 0.0)
        for q in range(8):
            for rb in range(4):
                slot = fbuf[:, (rb % 2) * 1024:(rb % 2) * 1024 + 1024]
                P.dma("sync", slot, W1f_d[q * 512 + rb * 128: q * 512 + (rb + 1) * 128, :],
                      tag=f"w1f q{q}")
                tern_tiles(slot, 0, 1024,
                           [wbuf[:, kc * 512 + rb * 128: kc * 512 + (rb + 1) * 128]
                            for kc in range(8)])
            for rb in range(8):
                slot = fbuf[:, 2048 + (rb % 2) * 512: 2048 + (rb % 2) * 512 + 512]
                P.dma("sync", slot, W2f_d[rb * 128:(rb + 1) * 128, q * 512:(q + 1) * 512],
                      tag=f"w2f q{q}")
                tern_tiles(slot, 2, 512,
                           [wbuf[:, 4096 + ffc * 1024 + rb * 128: 4096 + ffc * 1024 + (rb + 1) * 128]
                            for ffc in range(4)])
            for tt in range(2):
                tok0 = g * 512 + tt * 256
                for dc in range(4):
                    bank = PS[4 + dc % 2][:, 0:256]
                    for kc in range(8):
                        p_mm(P, bank, wbuf[:, kc * 512 + dc * 128: kc * 512 + dc * 128 + 128],
                             xbf[:, kc * TPC + tok0: kc * TPC + tok0 + 256],
                             start=(kc == 0), stop=(kc == 7), tag="ffn1")
                    p_act(P, h2[:, dc * 256:(dc + 1) * 256], bank, AF.Gelu,
                          bias=b1cols[:, (q * 4 + dc):(q * 4 + dc) + 1], tag="hgelu")
                for s2 in range(2):
                    for hf in range(2):
                        bank = PS[s2 * 2 + hf][:, 0:512]
                        for kc in range(4):
                            p_mm(P, bank,
                                 h2[:, kc * 256 + s2 * 128: kc * 256 + s2 * 128 + 128],
                                 wbuf[:, 4096 + kc * 1024 + hf * 512: 4096 + kc * 1024 + hf * 512 + 512],
                                 start=(kc == 0), stop=(kc == 3), tag="ffn2")
                        ya = ysm[:, (tt * 2 + s2) * 1024 + hf * 512:
                                 (tt * 2 + s2) * 1024 + hf * 512 + 512]
                        p_tt(P, ya, ya, bank, ALU.add, tag="yflushE")
        for st in range(4):
            st8 = g * 4 + st
            for c in range(8):
                p_trans(P, PSB[0][:, 0:128],
                        ysm[:, st * 1024 + c * 128: st * 1024 + c * 128 + 128],
                        identb[:], tag="ytrE")
                xs = x_sb[:, c * TPC + st8 * 128: c * TPC + st8 * 128 + 128]
                p_stt(P, xs, PSB[0][:, 0:128], b2cols[:, c:c + 1], xs,
                      ALU.add, ALU.add, tag="resE1")
    for c in range(8):
        P.dma("sync", out_d[c * 128:(c + 1) * 128, :], x_sb[:, c * TPC:(c + 1) * TPC],
              tag=f"stout{c}", out_final=True)
    nc2 = P.finalize()
    stack.close()
    return nc2, cp.vals


def _fused_inmaps(inp):
    f32 = np.float32
    ca = np.ascontiguousarray
    x0 = np.asarray(inp["L0_attn_x"], f32)
    _ = (inp["L1_attn_cos"], inp["L1_attn_sin"])
    ncF, cvF = _prog("F")
    consts = np.tile(np.array(list(cvF) + [0.0] * (16 - len(cvF)), f32), (128, 1))
    crow = np.ones((1, 128), f32)
    ident = np.eye(128, dtype=f32)
    cmask = np.tril(np.ones((128, 128), f32)).T.copy()
    onesb = np.ones((128, 1), ml_dtypes.bfloat16)
    cum32 = (np.arange(32)[:, None] <= np.arange(32)[None, :]).astype(f32)
    oh31 = np.zeros((32, 1), f32); oh31[31, 0] = 1.0
    order = (np.arange(96) % 32) * 3 + (np.arange(96) // 32)
    strict = (order[:, None] > order[None, :]).astype(f32)
    amask = ((np.arange(96)[:, None] % 32) <= np.arange(32)[None, :]).astype(f32)
    conv_q = np.asarray(inp["L1_attn_conv_q"], f32).reshape(8, 128, 4)
    conv_k = np.asarray(inp["L1_attn_conv_k"], f32).reshape(3, 8, 128, 4)
    conv_k = ca(conv_k.transpose(1, 0, 2, 3))
    conv_v = np.asarray(inp["L1_attn_conv_v"], f32).reshape(3, 8, 128, 4)
    conv_v = ca(conv_v.transpose(1, 0, 2, 3))
    in_maps = []
    for c in range(NCORE):
        b, s = c // 4, c % 4
        pmask = np.zeros((1, 8), f32)
        hmask = np.zeros((1, 8), f32)
        ohm = np.zeros((1, 8), f32); ohm[0, c] = 1.0
        for r in range(8):
            if r // 4 == b and r % 4 < s:
                pmask[0, r] = 1.0
        if s > 0:
            hmask[0, c - 1] = 1.0
        ohsel3 = np.zeros((3, 24), f32)
        for j in range(3):
            ohsel3[j, 3 * c + j] = 1.0
        hsel = np.zeros((24, 3), f32)
        for r in range(8):
            if hmask[0, r]:
                for j in range(3):
                    hsel[3 * r + j, j] = 1.0
        in_maps.append(dict(
            xT=ca(x0[b, s * TPC:(s + 1) * TPC].T),
            Wqkv=np.asarray(inp["L0_attn_Wqkv"], f32),
            bqkv=np.asarray(inp["L0_attn_bqkv"], f32),
            gq=np.asarray(inp["L0_attn_gamma_q"], f32),
            bq=np.asarray(inp["L0_attn_beta_q"], f32),
            gk=np.asarray(inp["L0_attn_gamma_k"], f32),
            bk=np.asarray(inp["L0_attn_beta_k"], f32),
            Wo0=np.asarray(inp["L0_attn_Wo"], f32), bo=np.asarray(inp["L0_attn_bo"], f32),
            gwT=ca(np.asarray(inp["L0_ffn_gate_w"], f32).T),
            gb=np.asarray(inp["L0_ffn_gate_b"], f32).reshape(1, 8),
            eb=np.asarray(inp["L0_ffn_expert_bias"], f32).reshape(1, 8),
            w1=np.asarray(inp["L0_ffn_w1"], f32), b1=np.asarray(inp["L0_ffn_b1"], f32),
            w2=np.asarray(inp["L0_ffn_w2"], f32), b2=np.asarray(inp["L0_ffn_b2"], f32),
            Wq=np.asarray(inp["L1_attn_Wq"], f32), Wk=np.asarray(inp["L1_attn_Wk"], f32),
            Wv=np.asarray(inp["L1_attn_Wv"], f32), Wb=np.asarray(inp["L1_attn_Wb"], f32),
            Wa=np.asarray(inp["L1_attn_Wa"], f32), Wg=np.asarray(inp["L1_attn_Wg"], f32),
            cqw=conv_q, ckw=conv_k, cvw=conv_v,
            alog=np.asarray(inp["L1_attn_A_log"], f32).reshape(1, 8),
            dtb=np.asarray(inp["L1_attn_dt_bias"], f32).reshape(1, 8),
            rmsw=np.asarray(inp["L1_attn_rms_weight"], f32),
            Wo1=np.asarray(inp["L1_attn_Wo"], f32),
            W1f=np.asarray(inp["L1_ffn_W1"], f32), b1f=np.asarray(inp["L1_ffn_b1"], f32),
            W2f=np.asarray(inp["L1_ffn_W2"], f32), b2f=np.asarray(inp["L1_ffn_b2"], f32),
            ident=ident, consts=consts, crow=crow, cmask=cmask, cum32=cum32,
            oh31=oh31, strict=strict, amask=amask, pmask=pmask, hmask=hmask,
            ohm=ohm, hsel=hsel, ohsel3=ohsel3, onesb=onesb))
    return ncF, in_maps


def kernel(**inp):
    ncF, in_maps = _fused_inmaps(inp)
    res = run_bass_kernel_spmd(ncF, in_maps, core_ids=list(range(NCORE)))
    out = np.empty((2, T, D), np.float32)
    for c in range(NCORE):
        b, s = c // 4, c % 4
        out[b, s * TPC:(s + 1) * TPC, :] = np.asarray(res.results[c]["out"]).T
    return out
